# version 06
# baseline (speedup 1.0000x reference)
import numpy as np
import concourse.bass as bass
import concourse.mybir as mybir
from concourse.bass_utils import run_bass_kernel_spmd

F32 = mybir.dt.float32
BF16 = mybir.dt.bfloat16
AF = mybir.ActivationFunctionType
ALU = mybir.AluOpType
AX = mybir.AxisListType.X

NCORE = 8
B, S, D, E, F = 4, 8192, 1024, 8, 4096
DC = D // 128
FCN = F // 128
TOWN = 4096
TA = 256
NTA = S // TA
TB = 512
NTB = TOWN // TB
TT = 512
NTT = TOWN // TT
EPS = 1e-5
KB = 1024


class Prog:
    ENGS = ("pe", "dve", "act", "gp", "sp")
    CHANS = ("pe", "dve", "act", "gp", "dxc", "dxa0", "dxa1", "dxb0", "dxb1",
             "dw1a", "dw1b", "dw2a", "dw2b", "dw2c", "dw2d", "do0", "do1", "do2", "do3", "dbg",
             "pre0", "pre1", "pre2", "pre3", "pre4", "pre5", "pre6", "pre7")

    def __init__(self, nc):
        self.nc = nc
        self.streams = {e: [] for e in self.ENGS}
        self.sems = {c: nc.alloc_semaphore(c) for c in self.CHANS}
        self.cnt = {c: 0 for c in self.CHANS}
        self.lastw = {}
        self.pend = {}
        self.eng_obj = {"pe": nc.tensor, "dve": nc.vector, "act": nc.scalar,
                        "gp": nc.gpsimd, "sp": nc.sync}

    def emit(self, eng, fn):
        self.streams[eng].append(fn)

    def wait(self, eng, ch, val):
        if val is None or val <= 0:
            return
        key = (eng, ch)
        if self.lastw.get(key, 0) >= val:
            return
        self.lastw[key] = val
        self.pend.setdefault(eng, []).append((self.sems[ch], val))

    def flushw(self, eng):
        pend = self.pend.pop(eng, [])
        if not pend:
            return
        E_ = self.eng_obj[eng]
        for sem, val in pend:
            self.emit(eng, lambda E_=E_, sem=sem, val=val: E_.wait_ge(sem, val))

    def run(self, eng, fn, inc=None, n=1, dma=False):
        if eng in ("dve", "act") and not dma:
            self.wait(eng, eng, self.cnt[eng])
            if inc is None:
                inc = eng
        pend = self.pend.pop(eng, [])
        E_ = self.eng_obj[eng]
        if inc is None:
            sem = None
        else:
            sem = self.sems[inc]
            self.cnt[inc] += n

        def closure(fn=fn, sem=sem, n=n, pend=pend, E_=E_):
            for s_, v_ in pend[:-1]:
                E_.wait_ge(s_, v_)
            i = fn()
            if pend:
                i._wait_ge(pend[-1][0], pend[-1][1])
            if sem is not None:
                i.then_inc(sem, n)
        self.emit(eng, closure)
        return self.cnt[inc] if inc is not None else None


def build(debug=False, last_phase="C", cstub=None):
    nc = bass.Bass("TRN2", target_bir_lowering=False)
    t = nc.tensor
    v = nc.vector
    s = nc.scalar
    g = nc.gpsimd
    sy = nc.sync

    xT = nc.dram_tensor("xT", [D, S], F32, kind="ExternalInput").ap()
    xTo = nc.dram_tensor("xTo", [D, TOWN], F32, kind="ExternalInput").ap()
    wqT = nc.dram_tensor("wqT", [D, D], F32, kind="ExternalInput").ap()
    wkT = nc.dram_tensor("wkT", [D, D], F32, kind="ExternalInput").ap()
    wvT = nc.dram_tensor("wvT", [D, D], F32, kind="ExternalInput").ap()
    woT = nc.dram_tensor("woT", [D, D], F32, kind="ExternalInput").ap()
    bqh = nc.dram_tensor("bq", [1, D], F32, kind="ExternalInput").ap()
    bkh = nc.dram_tensor("bk", [1, D], F32, kind="ExternalInput").ap()
    bvh = nc.dram_tensor("bv", [1, D], F32, kind="ExternalInput").ap()
    boh = nc.dram_tensor("bo", [1, D], F32, kind="ExternalInput").ap()
    wrTh = nc.dram_tensor("wrT", [E, D], F32, kind="ExternalInput").ap()
    w1h = nc.dram_tensor("w1", [E, D, F], F32, kind="ExternalInput").ap()
    b1h = nc.dram_tensor("b1", [E, F], F32, kind="ExternalInput").ap()
    w2h = nc.dram_tensor("w2", [E, F, D], F32, kind="ExternalInput").ap()
    b2h = nc.dram_tensor("b2", [E, D], F32, kind="ExternalInput").ap()
    y = nc.dram_tensor("y", [TOWN, D], F32, kind="ExternalOutput").ap()
    w1s = nc.dram_tensor("w1s", [E, D, F], BF16, kind="Internal").ap()
    w2s = nc.dram_tensor("w2s", [E, F, D], BF16, kind="Internal").ap()

    dbg = {}
    if debug:
        dbg["kv"] = nc.dram_tensor("dbg_kv", [128, DC * D], F32, kind="ExternalOutput").ap()
        dbg["ksum"] = nc.dram_tensor("dbg_ksum", [128, 8], F32, kind="ExternalOutput").ap()
        if last_phase in ("B", "C"):
            dbg["h"] = nc.dram_tensor("dbg_h", [128, DC * TOWN], F32, kind="ExternalOutput").ap()
            dbg["gate"] = nc.dram_tensor("dbg_gate", [128, 256], F32, kind="ExternalOutput").ap()
        if last_phase == "C":
            dbg["hfm"] = nc.dram_tensor("dbg_hfm", [128, DC * TOWN], F32, kind="ExternalOutput").ap()
            dbg["htm"] = nc.dram_tensor("dbg_htm", [128, D], F32, kind="ExternalOutput").ap()
            dbg["hin"] = nc.dram_tensor("dbg_hin", [128, 4 * D], F32, kind="ExternalOutput").ap()
            dbg["acc"] = nc.dram_tensor("dbg_acc", [128, 4 * D], F32, kind="ExternalOutput").ap()
            dbg["h1g"] = nc.dram_tensor("dbg_h1g", [128, FCN * 512], F32, kind="ExternalOutput").ap()
            dbg["w1sl"] = nc.dram_tensor("dbg_w1sl", [128, 8 * KB], F32, kind="ExternalOutput").ap()

    SB0 = 16512
    def place(name, shape, dtype, off):
        return nc.alloc_sbuf_tensor_at(name, shape, dtype, offset=SB0 + off)

    off_a = 0

    def pa(name, shape, dtype):
        nonlocal off_a
        tsz = int(np.prod(shape[1:])) * mybir.dt.size(dtype)
        tns = place(name, shape, dtype, off_a)
        off_a += (tsz + 63) & ~63
        return tns

    ones_row = pa("ones_row", [1, 512], BF16)
    ones_col = pa("ones_col", [128, 1], BF16)
    ident_bf = pa("ident_bf", [128, 128], BF16)
    ident_f32 = pa("ident_f32", [128, 128], F32)
    bq_col = pa("bq_col", [128, 8], F32)
    bqn_col = pa("bqn_col", [128, 8], F32)
    bo_col = pa("bo_col", [128, 8], F32)
    wr_bf = pa("wr_bf", [128, 64], BF16)
    b2_sb = pa("b2_sb", [8, D], BF16)
    ksum_bf = pa("ksum_bf", [128, 8], BF16)
    b1c = pa("b1c", [128, 256], F32)
    gate_tm = pa("gate_tm", [128, 256], F32)
    gate_fm = pa("gate_fm", [8, TOWN], BF16)
    rt_nm = pa("rt_nm", [128, 8], F32)
    rt_ex = pa("rt_ex", [128, 32], F32)
    rt_g1 = pa("rt_g1", [128, 32], F32)
    rt_gb = pa("rt_gb", [128, 32], BF16)
    ln2r = pa("ln2r", [128, 16], F32)
    ksf = pa("ksf", [128, 8], F32)
    eps_col = pa("eps_col", [128, 1], F32)
    obuf1 = pa("obuf1", [128, D], F32)
    assert off_a <= 22528, off_a

    W0 = 22528
    wq_bf = place("wq_bf", [128, DC * D], BF16, W0)
    wo_bf = place("wo_bf", [128, DC * D], BF16, W0 + 16 * KB)
    wk_bf = place("wk_bf", [128, DC * D], BF16, W0 + 32 * KB)
    wv_bf = place("wv_bf", [128, DC * D], BF16, W0 + 48 * KB)
    kv_bf = place("kv_bf", [128, DC * D], BF16, W0 + 32 * KB)
    w1sl = [place(f"w1sl{i}", [128, 8 * KB], BF16, W0 + i * 16 * KB) for i in range(2)]
    w2sl = [place(f"w2sl{i}", [128, 4 * KB], BF16, W0 + 32 * KB + i * 8 * KB) for i in range(4)]

    H0 = 88064
    h_fm = place("h_fm", [128, DC * TOWN], BF16, H0)
    b1_nat = place("b1_nat", [8, F], F32, H0)
    bqrow = place("bqrow", [1, D], F32, H0 + 16 * KB)
    borow = place("borow", [1, D], F32, H0 + 20 * KB)
    wrT_bf = place("wrT_bf", [8, D], BF16, H0 + 24 * KB)
    bkrow = place("bkrow", [1, D], BF16, H0 + 26 * KB)
    bvrow = place("bvrow", [1, D], BF16, H0 + 28 * KB)

    D0 = 153600
    xbB = [place(f"xbB{i}", [128, DC * TB], BF16, D0 + i * 8 * KB) for i in range(2)]
    qs_bf = place("qs_bf", [128, DC * TB], BF16, D0 + 16 * KB)
    ctx_bf = place("ctx_bf", [128, DC * TB], BF16, D0 + 24 * KB)
    h1g = place("h1g", [128, FCN * 512], BF16, D0)

    E0 = 186368
    kv_acc = place("kv_acc", [128, DC * D], F32, D0)
    xbA = [place(f"xbA{i}", [128, DC * TA], BF16, E0 + i * 4 * KB) for i in range(2)]
    ktm = place("ktm", [128, 2 * D], BF16, E0 + 8 * KB)
    vtm = place("vtm", [128, 2 * D], BF16, E0 + 12 * KB)
    elu_a = place("elu_a", [128, 512], F32, E0 + 16 * KB)
    elu_m = place("elu_m", [128, 512], F32, E0 + 18 * KB)
    v_bf = place("v_bf", [128, DC * TB], BF16, E0)
    r_m = place("r_m", [1, 512], F32, E0 + 8 * KB)
    r_v = place("r_v", [1, 512], F32, E0 + 10 * KB)
    v2_bf = place("v2_bf", [128, 2 * 512], BF16, E0 + 12 * KB)
    r_t = place("r_t", [1, 512], F32, E0 + 14 * KB)
    arow_bf = place("arow_bf", [1, 512], BF16, E0 + 20 * KB)
    brow_bf = place("brow_bf", [1, 512], BF16, E0 + 21 * KB)
    zrow_bf = place("zrow_bf", [1, 512], BF16, E0 + 22 * KB)
    zb_sb = place("zb_sb", [128, 512], F32, E0 + 23 * KB)
    acc = place("acc", [128, 4 * D], F32, E0)
    h_tm = [place(f"h_tm{i}", [128, D], BF16, E0 + 16 * KB + i * 2 * KB) for i in range(2)]
    obuf0 = place("obuf0", [128, D], F32, E0 + 20 * KB)
    assert 16512 + E0 + 24 * KB + 64 <= 229344

    pb = [nc.alloc_psum_tensor(f"pb{i}", [128, 512], F32) for i in range(6)]
    pkv = nc.alloc_psum_tensor("pkv", [128, 1024], F32)
    pkv6 = pkv[:, 0:512]
    pkv7 = pkv[:, 512:1024]

    P = Prog(nc)
    bank_rel = {}

    def bank_wait(key):
        for ch, val in bank_rel.get(key, ()):
            P.wait("pe", ch, val)

    def release(key, ch, val, add=False):
        if add and key in bank_rel:
            bank_rel[key] = bank_rel[key] + [(ch, val)]
        else:
            bank_rel[key] = [(ch, val)]

    def pe_group(key, mms, inc=True):
        bank_wait(key)
        n = len(mms)
        for i, fn in enumerate(mms):
            P.run("pe", fn, inc="pe" if (i == n - 1 and inc) else None)
        return P.cnt["pe"]

    elu_guard = [0]

    xa_slot_rel = [0, 0]
    dx_xa = []

    def emit_load_xa(k):
        sl = k % 2
        P.wait("gp", "pe", xa_slot_rel[sl])
        t0 = k * TA
        ch = f"dxa{sl}"
        return ch, P.run("gp", lambda t0=t0, sl=sl: g.dma_start(
            xbA[sl][:, :].rearrange("p (c n) -> p c n", c=DC),
            xT[:, t0:t0 + TA].rearrange("(c p) n -> p c n", p=128)),
            inc=ch, n=16)

    def emit_init():
        gpv = P.run("gp", lambda: g.memset(ident_bf[:, :], 0.0), inc="gp")
        P.wait("gp", "gp", gpv)
        P.run("gp", lambda: g.affine_select(out=ident_bf[:, :], in_=ident_bf[:, :],
                                            compare_op=ALU.not_equal, fill=1.0, base=0,
                                            pattern=[[-1, 128]], channel_multiplier=1))
        gpv = P.run("gp", lambda: g.memset(ident_f32[:, :], 0.0), inc="gp")
        P.wait("gp", "gp", gpv)
        P.run("gp", lambda: g.affine_select(out=ident_f32[:, :], in_=ident_f32[:, :],
                                            compare_op=ALU.not_equal, fill=1.0, base=0,
                                            pattern=[[-1, 128]], channel_multiplier=1))
        P.run("gp", lambda: g.memset(ones_row[:, :], 1.0))
        P.run("gp", lambda: g.memset(ones_col[:, :], 1.0))
        gp_done = P.run("gp", lambda: g.memset(eps_col[:, :], EPS), inc="gp")

        def cast_w(dst, src):
            return P.run("gp", lambda dst=dst, src=src: g.dma_start(
                dst[:, :].rearrange("p (c o) -> p c o", c=DC),
                src.rearrange("(c p) o -> p c o", p=128)), inc="dxc", n=16)

        dx_wk = cast_w(wk_bf, wkT)
        dx_wv = cast_w(wv_bf, wvT)
        dx_xa.append(emit_load_xa(0))
        dx_wq = cast_w(wq_bf, wqT)
        dx_xa.append(emit_load_xa(1))
        dx_wo = cast_w(wo_bf, woT)
        P.run("gp", lambda: g.dma_start(bkrow[:, :], bkh), inc="dxc", n=16)
        P.run("gp", lambda: g.dma_start(bvrow[:, :], bvh), inc="dxc", n=16)
        P.run("gp", lambda: g.dma_start(bqrow[:, :], bqh), inc="dxc", n=16)
        P.run("gp", lambda: g.dma_start(borow[:, :], boh), inc="dxc", n=16)
        P.run("gp", lambda: g.dma_start(wrT_bf[:, :], wrTh), inc="dxc", n=16)
        P.run("gp", lambda: g.dma_start(b2_sb[:, :], b2h), inc="dxc", n=16)
        dx_rows = P.run("gp", lambda: g.dma_start(b1_nat[:, :], b1h), inc="dxc", n=16)

        P.wait("pe", "gp", gp_done)
        P.wait("pe", "dxc", dx_rows)
        for dq in range(DC):
            c1 = pe_group("pb0", [
                lambda dq=dq: t.transpose(pb[0][:, 0:1], bqrow[0:1, dq * 128:(dq + 1) * 128],
                                          ident_f32[0:1, 0:1]),
                lambda dq=dq: t.transpose(pb[0][:, 1:2], borow[0:1, dq * 128:(dq + 1) * 128],
                                          ident_f32[0:1, 0:1])])
            P.wait("dve", "pe", c1)
            P.run("dve", lambda dq=dq: v.tensor_copy(bq_col[:, dq:dq + 1], pb[0][:, 0:1]))
            dv = P.run("dve", lambda dq=dq: v.tensor_copy(bo_col[:, dq:dq + 1], pb[0][:, 1:2]),
                       inc="dve")
            release("pb0", "dve", dv)
        for dc in range(DC):
            c1 = pe_group("pb1", [
                lambda dc=dc: t.transpose(pb[1][:, :].bitcast(BF16)[:, 0:8],
                                          wrT_bf[0:8, dc * 128:(dc + 1) * 128],
                                          ident_bf[0:8, 0:8])])
            P.wait("dve", "pe", c1)
            dv = P.run("dve", lambda dc=dc: v.tensor_copy(
                wr_bf[:, dc * 8:(dc + 1) * 8], pb[1][:, :].bitcast(BF16)[:, 0:8]), inc="dve")
            release("pb1", "dve", dv)
        for fc in range(FCN):
            c1 = pe_group("pb2", [
                lambda fc=fc: t.transpose(pb[2][:, 0:8], b1_nat[0:8, fc * 128:(fc + 1) * 128],
                                          ident_f32[0:8, 0:8])])
            P.wait("dve", "pe", c1)
            dv = P.run("dve", lambda fc=fc: v.tensor_copy(
                b1c[:, fc * 8:(fc + 1) * 8], pb[2][:, 0:8]), inc="dve")
            release("pb2", "dve", dv)
        bqn_done = P.run("dve", lambda: v.tensor_scalar_mul(bqn_col[:, :], bq_col[:, :], -1.0),
                         inc="dve")
        return dict(dx_rows=dx_rows, bqn_done=bqn_done)

    prepass_units = []
    for e_ in range(E):
        for q_ in range(4):
            prepass_units.append(lambda e=e_, q=q_: g.dma_start(
                w1s[e, :, q * 1024:(q + 1) * 1024], w1h[e, :, q * 1024:(q + 1) * 1024]))
            prepass_units.append(lambda e=e_, q=q_: g.dma_start(
                w2s[e, q * 1024:(q + 1) * 1024, :], w2h[e, q * 1024:(q + 1) * 1024, :]))
    prepass_done_at = {}
    prepass_idx = [0]

    def emit_prepass(nmax):
        for _ in range(nmax):
            i = prepass_idx[0]
            if i >= len(prepass_units):
                return
            e = i // 8
            cnt = P.run("gp", prepass_units[i], inc=f"pre{e}", n=16)
            prepass_idx[0] += 1
            if i % 8 == 7:
                prepass_done_at[e] = cnt

    def emit_phase_a(ii):
        ks_started = [False]
        kv_slot_ctr = [0]
        kv_slots = [(pb[4][:, :], "pb4"), (pkv6[:, :], "pkv6"), (pkv7[:, :], "pkv7")]
        for k in range(NTA):
            sl = k % 2
            P.wait("pe", *dx_xa[k])
            if k == 0:
                P.wait("pe", "dxc", ii["dx_rows"])
            epil = []
            for ts in range(2):
                for proj in ("k", "v"):
                    wsb = wk_bf if proj == "k" else wv_bf
                    brw = bkrow if proj == "k" else bvrow
                    dst = ktm if proj == "k" else vtm
                    for nh in range(2):
                        bi = (0 if proj == "k" else 2) + nh
                        key = f"pb{bi}"
                        mms = [lambda dc=dc, nh=nh, wsb=wsb, bi=bi, ts=ts, sl=sl:
                               t.matmul(pb[bi][:, :],
                                        xbA[sl][:, dc * TA + ts * 128:dc * TA + ts * 128 + 128],
                                        wsb[:, dc * D + nh * 512:dc * D + nh * 512 + 512],
                                        start=(dc == 0), stop=False)
                               for dc in range(DC)]
                        mms.append(lambda nh=nh, brw=brw, bi=bi:
                                   t.matmul(pb[bi][:, :], ones_row[:, 0:128],
                                            brw[:, nh * 512:nh * 512 + 512],
                                            start=False, stop=True))
                        pe_c = pe_group(key, mms)
                        if proj == "v":
                            P.wait("dve", "pe", pe_c)
                            dv = P.run("dve", lambda bi=bi, ts=ts, nh=nh, dst=dst:
                                       v.tensor_copy(dst[:, ts * D + nh * 512:ts * D + nh * 512 + 512],
                                                     pb[bi][:, :]), inc="dve")
                            release(key, "dve", dv)
                            epil.append(dv)
                        else:
                            P.wait("act", "pe", pe_c)
                            P.wait("act", "dve", elu_guard[0])
                            P.run("act", lambda bi=bi: s.activation(elu_a[:, :], pb[bi][:, :],
                                                                    AF.Relu))
                            ac = P.run("act", lambda bi=bi: s.activation(
                                elu_m[:, :], pb[bi][:, :], AF.Relu, scale=-1.0), inc="act")
                            release(key, "act", ac)
                            ac = P.run("act", lambda: s.activation(
                                elu_m[:, :], elu_m[:, :], AF.Exp, scale=-1.0), inc="act")
                            P.wait("dve", "act", ac)
                            dv = P.run("dve", lambda ts=ts, nh=nh, dst=dst:
                                       v.tensor_add(dst[:, ts * D + nh * 512:ts * D + nh * 512 + 512],
                                                    elu_a[:, :], elu_m[:, :]), inc="dve")
                            elu_guard[0] = dv
                            epil.append(dv)
            for dvv in epil:
                P.wait("pe", "dve", dvv)
            for c in range(DC):
                for half in range(2):
                    si = kv_slot_ctr[0] % 3
                    kv_slot_ctr[0] += 1
                    bank_ap, key = kv_slots[si]
                    mms = [lambda c=c, half=half, ts=ts, bank_ap=bank_ap:
                           t.matmul(bank_ap,
                                    ktm[:, ts * D + c * 128:ts * D + c * 128 + 128],
                                    vtm[:, ts * D + half * 512:ts * D + half * 512 + 512],
                                    start=(ts == 0), stop=(ts == 1))
                           for ts in range(2)]
                    if half == 0:
                        for ts in range(2):
                            st = not ks_started[0]
                            ks_started[0] = True
                            last_ks = (k == NTA - 1 and c == DC - 1 and ts == 1)
                            mms.append(lambda c=c, ts=ts, st=st, last_ks=last_ks:
                                       t.matmul(pb[5][:, c:c + 1],
                                                ktm[:, ts * D + c * 128:ts * D + c * 128 + 128],
                                                ones_col[:, :], start=st, stop=last_ks))
                    pe_c = pe_group(key, mms)
                    P.wait("dve", "pe", pe_c)
                    dst = kv_acc[:, c * D + half * 512:c * D + half * 512 + 512]
                    if k == 0:
                        dv = P.run("dve", lambda dst=dst, bank_ap=bank_ap:
                                   v.tensor_copy(dst, bank_ap), inc="dve")
                    else:
                        dv = P.run("dve", lambda dst=dst, bank_ap=bank_ap:
                                   v.tensor_add(dst, dst, bank_ap), inc="dve")
                    release(key, "dve", dv)
            xa_slot_rel[sl] = P.cnt["pe"]
            if k + 2 < NTA:
                dx_xa.append(emit_load_xa(k + 2))
            emit_prepass(2)
        pe_c = P.cnt["pe"]
        P.wait("dve", "pe", pe_c)
        P.run("dve", lambda: v.tensor_scalar(ksf[:, :], pb[5][:, 0:8], 1e-6, None,
                                             ALU.add, ALU.bypass))
        dv = P.run("dve", lambda: v.tensor_copy(ksum_bf[:, :], ksf[:, :]), inc="dve")
        release("pb5", "dve", dv)
        for c in range(DC):
            dv = P.run("dve", lambda c=c: v.tensor_copy(
                kv_bf[:, c * D:(c + 1) * D], kv_acc[:, c * D:(c + 1) * D]), inc="dve")
        kv_ready = dv
        if debug:
            P.wait("gp", "dve", kv_ready)
            P.run("gp", lambda: g.dma_start(dbg["kv"], kv_acc[:, :]), inc="dbg", n=16)
            dbg_done = P.run("gp", lambda: g.dma_start(dbg["ksum"], ksf[:, :]), inc="dbg", n=16)
            dbg["a_done"] = dbg_done
        return kv_ready

    xb_rel = [0, 0]
    dx_xb = []

    def emit_load_xb(j):
        sl = j % 2
        P.wait("gp", "dve", xb_rel[sl])
        t0 = j * TB
        ch = f"dxb{sl}"
        return ch, P.run("gp", lambda t0=t0, sl=sl: g.dma_start(
            xbB[sl][:, :].rearrange("p (c n) -> p c n", c=DC),
            xTo[:, t0:t0 + TB].rearrange("(c p) n -> p c n", p=128)),
            inc=ch, n=16)

    def emit_phase_b(ii, kv_ready):
        P.wait("gp", "dve", kv_ready)
        dx_xb.append(emit_load_xb(0))
        dx_xb.append(emit_load_xb(1))
        P.wait("pe", "dve", kv_ready)
        P.wait("act", "dve", ii["bqn_done"])
        h_last_map = {}

        def emit_router(j, h_last):
                P.wait("pe", "dve", h_last)
                mms = []
                for tl in range(4):
                    tcol = j * TB + tl * 128
                    for c in range(DC):
                        mms.append(lambda c=c, tcol=tcol, tl=tl: t.matmul(
                            pb[5][:, tl * 8:tl * 8 + 8],
                            h_fm[:, c * TOWN + tcol:c * TOWN + tcol + 128],
                            wr_bf[:, c * 8:c * 8 + 8], start=(c == 0), stop=(c == DC - 1)))
                pe_c = pe_group("pb5", mms)
                P.wait("dve", "pe", pe_c)
                rel_list = []
                for tl in range(4):
                    dv = P.run("dve", lambda tl=tl: v.reduce_max(
                        rt_nm[:, tl:tl + 1], pb[5][:, tl * 8:tl * 8 + 8], axis=AX, negate=True),
                        inc="dve")
                    P.wait("act", "dve", dv)
                    ac = P.run("act", lambda tl=tl: s.activation(
                        rt_ex[:, tl * 8:tl * 8 + 8], pb[5][:, tl * 8:tl * 8 + 8], AF.Exp,
                        bias=rt_nm[:, tl:tl + 1]), inc="act")
                    P.wait("dve", "act", ac)
                    dv = P.run("dve", lambda tl=tl: v.reduce_sum(
                        rt_nm[:, 4 + tl:5 + tl], rt_ex[:, tl * 8:tl * 8 + 8], axis=AX), inc="dve")
                    dv = P.run("dve", lambda tl=tl: v.reciprocal(
                        rt_nm[:, 4 + tl:5 + tl], rt_nm[:, 4 + tl:5 + tl]), inc="dve")
                    dvq = P.run("dve", lambda tl=tl: v.tensor_scalar(
                        rt_g1[:, tl * 8:tl * 8 + 8], pb[5][:, tl * 8:tl * 8 + 8],
                        rt_nm[:, tl:tl + 1], 0.0, ALU.add, ALU.is_equal), inc="dve")
                    gcol0 = (j * 4 + tl) * 8
                    dv = P.run("dve", lambda gcol0=gcol0, tl=tl: v.tensor_scalar(
                        gate_tm[:, gcol0:gcol0 + 8], rt_g1[:, tl * 8:tl * 8 + 8],
                        rt_nm[:, 4 + tl:5 + tl], None, ALU.mult, ALU.bypass), inc="dve")
                    rel_list.append(("act", ac))
                    rel_list.append(("dve", dvq))
                bank_rel["pb5"] = rel_list[-2:] + [max(
                    (cv for cv in rel_list if cv[0] == "act"), key=lambda x: x[1]),
                    max((cv for cv in rel_list if cv[0] == "dve"), key=lambda x: x[1])]

        for j in range(NTB):
            sl = j % 2
            P.wait("pe", *dx_xb[j])
            q_last = 0
            for dq in range(DC):
                bi = dq % 2
                key = f"pb{bi}"
                mms = [lambda dc=dc, dq=dq, sl=sl, bi=bi:
                       t.matmul(pb[bi][:, :], wq_bf[:, dc * D + dq * 128:dc * D + dq * 128 + 128],
                                xbB[sl][:, dc * TB:dc * TB + 512],
                                start=(dc == 0), stop=(dc == DC - 1))
                       for dc in range(DC)]
                pe_c = pe_group(key, mms)
                P.wait("act", "pe", pe_c)
                P.wait("act", "dve", elu_guard[0])
                P.run("act", lambda bi=bi, dq=dq: s.activation(
                    elu_a[:, :], pb[bi][:, :], AF.Relu, bias=bq_col[:, dq:dq + 1]))
                ac = P.run("act", lambda bi=bi, dq=dq: s.activation(
                    elu_m[:, :], pb[bi][:, :], AF.Relu, scale=-1.0, bias=bqn_col[:, dq:dq + 1]),
                    inc="act")
                release(key, "act", ac)
                ac = P.run("act", lambda: s.activation(elu_m[:, :], elu_m[:, :], AF.Exp,
                                                       scale=-1.0), inc="act")
                P.wait("dve", "act", ac)
                dv = P.run("dve", lambda dq=dq: v.tensor_add(
                    qs_bf[:, dq * TB:dq * TB + 512], elu_a[:, :], elu_m[:, :]), inc="dve")
                elu_guard[0] = dv
                q_last = dv
            P.wait("pe", "dve", q_last)
            pe_c = pe_group("pkv6", [
                lambda c=c: t.matmul(pkv6[0:1, :], ksum_bf[:, c:c + 1],
                                     qs_bf[:, c * TB:c * TB + 512],
                                     start=(c == 0), stop=(c == DC - 1)) for c in range(DC)])
            P.wait("dve", "pe", pe_c)
            dv = P.run("dve", lambda: v.reciprocal(r_t[:, :], pkv6[0:1, :]), inc="dve")
            release("pkv6", "dve", dv)
            zr_done = P.run("dve", lambda: v.tensor_copy(zrow_bf[:, :], r_t[:, :]), inc="dve")
            zb_done = None
            ctx_last = 0
            for dvc in range(DC):
                bi = 2 + dvc % 2
                key = f"pb{bi}"
                mms = [lambda c=c, dvc=dvc, bi=bi:
                       t.matmul(pb[bi][:, :], kv_bf[:, c * D + dvc * 128:c * D + dvc * 128 + 128],
                                qs_bf[:, c * TB:c * TB + 512],
                                start=(c == 0), stop=(c == DC - 1)) for c in range(DC)]
                pe_c = pe_group(key, mms)
                if dvc == 0:
                    P.wait("pe", "dve", zr_done)
                    pe_z = pe_group("pkv7", [lambda: t.matmul(
                        pkv7[:, :], ones_row[:, 0:128], zrow_bf[:, :],
                        start=True, stop=True)])
                    P.wait("dve", "pe", pe_z)
                    zb_done = P.run("dve", lambda: v.tensor_copy(zb_sb[:, :], pkv7[:, :]),
                                    inc="dve")
                    release("pkv7", "dve", zb_done)
                P.wait("dve", "pe", pe_c)
                dv = P.run("dve", lambda bi=bi, dvc=dvc: v.tensor_copy(
                    ctx_bf[:, dvc * TB:dvc * TB + 512], pb[bi][:, :]), inc="dve")
                release(key, "dve", dv)
                ctx_last = dv
            P.wait("pe", "dve", ctx_last)
            v_last = 0
            for do in range(DC):
                bi = 4 + do % 2
                key = f"pb{bi}"
                mms = [lambda c=c, do=do, bi=bi:
                       t.matmul(pb[bi][:, :], wo_bf[:, c * D + do * 128:c * D + do * 128 + 128],
                                ctx_bf[:, c * TB:c * TB + 512],
                                start=(c == 0), stop=(c == DC - 1)) for c in range(DC)]
                pe_c = pe_group(key, mms)
                P.wait("dve", "pe", pe_c)
                P.wait("dve", "dve", elu_guard[0])
                P.run("dve", lambda bi=bi: v.tensor_mul(elu_a[:, :], pb[bi][:, :], zb_sb[:, :]))
                dv = P.run("dve", lambda do=do, sl=sl: v.scalar_tensor_tensor(
                    v_bf[:, do * TB:do * TB + 512], elu_a[:, :], bo_col[:, do:do + 1],
                    xbB[sl][:, do * TB:do * TB + 512], ALU.add, ALU.add), inc="dve")
                release(key, "dve", dv)
                v_last = dv
            xb_rel[sl] = v_last
            if j + 2 < NTB:
                dx_xb.append(emit_load_xb(j + 2))
            P.wait("pe", "dve", v_last)
            pe_s = pe_group("pkv6", [
                lambda c=c: t.matmul(pkv6[0:1, :], ones_col[:, :],
                                     v_bf[:, c * TB:c * TB + 512],
                                     start=(c == 0), stop=(c == DC - 1)) for c in range(DC)])
            bank_wait("pkv7")
            v2_pe = [0, 0]
            for c in range(DC):
                P.wait("dve", "pe", v2_pe[c % 2])
                dv = P.run("dve", lambda c=c: v.tensor_mul(
                    v2_bf[:, (c % 2) * 512:(c % 2) * 512 + 512],
                    v_bf[:, c * TB:c * TB + 512], v_bf[:, c * TB:c * TB + 512]), inc="dve")
                P.wait("pe", "dve", dv)
                pe_c = P.run("pe", lambda c=c: t.matmul(
                    pkv7[0:1, :], ones_col[:, :], v2_bf[:, (c % 2) * 512:(c % 2) * 512 + 512],
                    start=(c == 0), stop=(c == DC - 1)), inc="pe")
                v2_pe[c % 2] = pe_c
            pe_s2 = P.cnt["pe"]
            P.wait("dve", "pe", pe_s2)
            P.run("dve", lambda: v.tensor_scalar(r_m[:, :], pkv6[0:1, :], 1.0 / D, None,
                                                 ALU.mult, ALU.bypass))
            dv = P.run("dve", lambda: v.tensor_scalar(r_v[:, :], pkv7[0:1, :], 1.0 / D, None,
                                                      ALU.mult, ALU.bypass), inc="dve")
            release("pkv6", "dve", dv)
            release("pkv7", "dve", dv)
            P.wait("dve", "dve", dv)
            P.run("dve", lambda: v.tensor_mul(r_t[:, :], r_m[:, :], r_m[:, :]))
            dv = P.run("dve", lambda: v.tensor_sub(r_v[:, :], r_v[:, :], r_t[:, :]), inc="dve")
            P.wait("act", "dve", dv)
            P.run("act", lambda: s.activation(r_t[:, :], r_v[:, :], AF.Ln, bias=eps_col[0:1, :]))
            ac = P.run("act", lambda: s.activation(r_t[:, :], r_t[:, :], AF.Exp, scale=-0.5),
                       inc="act")
            P.wait("dve", "act", ac)
            dv = P.run("dve", lambda: v.tensor_copy(arow_bf[:, :], r_t[:, :]), inc="dve")
            P.wait("dve", "dve", dv)
            P.run("dve", lambda: v.tensor_mul(r_t[:, :], r_t[:, :], r_m[:, :]))
            dv2 = P.run("dve", lambda: v.tensor_scalar(brow_bf[:, :], r_t[:, :], -1.0, None,
                                                       ALU.mult, ALU.bypass), inc="dve")
            P.wait("pe", "dve", dv2)
            pe_group("pb2", [lambda: t.matmul(pb[2][:, :], ones_row[:, 0:128], arow_bf[:, :],
                                              start=True, stop=True)])
            pe_bb = pe_group("pb3", [lambda: t.matmul(pb[3][:, :], ones_row[:, 0:128],
                                                      brow_bf[:, :], start=True, stop=True)])
            P.wait("dve", "pe", pe_bb)
            P.wait("dve", "dve", elu_guard[0])
            h_last = 0
            for c in range(DC):
                P.run("dve", lambda c=c: v.tensor_mul(elu_a[:, :],
                                                      v_bf[:, c * TB:c * TB + 512], pb[2][:, :]))
                dv = P.run("dve", lambda c=c, j=j: v.tensor_add(
                    h_fm[:, c * TOWN + j * TB:c * TOWN + j * TB + 512], elu_a[:, :],
                    pb[3][:, :]), inc="dve")
                h_last = dv
            elu_guard[0] = h_last
            release("pb2", "dve", h_last)
            release("pb3", "dve", h_last)
            h_last_map[j] = h_last
            if j > 0:
                emit_router(j - 1, h_last_map[j - 1])
        emit_router(NTB - 1, h_last_map[NTB - 1])
        b_done = P.cnt["dve"]
        if debug:
            P.wait("gp", "dve", b_done)
            P.run("gp", lambda: g.dma_start(dbg["h"], h_fm[:, :]), inc="dbg", n=16)
            P.run("gp", lambda: g.dma_start(dbg["gate"], gate_tm[:, :]), inc="dbg", n=16)
        return b_done

    def emit_phase_c(b_done):
        pe_end_b = P.cnt["pe"]
        w1q = [(tt, e, fb) for tt in range(NTT) for e in range(E) for fb in range(4)]
        w2q = [(tt, e, dh, q) for tt in range(NTT) for e in range(E)
               for dh in range(2) for q in range(4)]
        w1_rel = [pe_end_b, pe_end_b]
        w2_rel = [pe_end_b] * 4
        w1_at = {}
        w2_at = {}
        w1_next = [0]
        w2_next = [0]

        def issue_w1():
            i = w1_next[0]
            if i >= len(w1q):
                return
            w1_next[0] += 1
            tt, e, fb = w1q[i]
            sl = i % 2
            ch = "dw1a" if sl == 0 else "dw1b"
            P.wait("act", "pe", w1_rel[sl])
            P.wait("act", f"pre{e}", prepass_done_at[e])
            c = P.run("act", lambda e=e, fb=fb, sl=sl: s.dma_start(
                w1sl[sl][:, :].rearrange("p (c f) -> p c f", c=DC),
                w1s[e, :, fb * 1024:(fb + 1) * 1024].rearrange("(c p) f -> p c f", p=128)),
                inc=ch, n=16, dma=True)
            w1_at[(tt, e, fb)] = (ch, c, sl)

        def issue_w2():
            i = w2_next[0]
            if i >= len(w2q):
                return
            w2_next[0] += 1
            tt, e, dh, q = w2q[i]
            sl = i % 4
            ch = "dw2" + "abcd"[sl]
            P.wait("sp", "pe", w2_rel[sl])
            P.wait("sp", f"pre{e}", prepass_done_at[e])
            c = P.run("sp", lambda e=e, dh=dh, q=q, sl=sl: sy.dma_start(
                w2sl[sl][:, :].rearrange("p (c d) -> p c d", c=8),
                w2s[e, q * 1024:(q + 1) * 1024, dh * 512:(dh + 1) * 512]
                .rearrange("(c p) d -> p c d", p=128)), inc=ch, n=16)
            w2_at[(tt, e, dh, q)] = (ch, c, sl)

        issue_w1()
        issue_w1()
        for _ in range(4):
            issue_w2()
        oslot_rel = [0, 0]

        def h1_phase(tt, e, inject=None):
            gelu_last = 0
            for fc in range(FCN):
                if inject and fc in (4, 12, 20, 28):
                    inject[(fc - 4) // 8]()
                fb = fc // 8
                fcl = fc % 8
                bi = fc % 4
                key = f"pb{bi}"
                wch, dwc, sl = w1_at[(tt, e, fb)]
                P.wait("pe", wch, dwc)
                mms = [lambda c=c, fcl=fcl, sl=sl, bi=bi, tt=tt: t.matmul(
                    pb[bi][:, :], w1sl[sl][:, c * 1024 + fcl * 128:c * 1024 + fcl * 128 + 128],
                    h_fm[:, c * TOWN + tt * TT:c * TOWN + tt * TT + 512],
                    start=(c == 0), stop=(c == DC - 1)) for c in range(DC)]
                pe_c = pe_group(key, mms)
                P.wait("act", "pe", pe_c)
                bcol = fc * 8 + e
                ac = P.run("act", lambda bi=bi, fc=fc, bcol=bcol: s.activation(
                    h1g[:, fc * 512:fc * 512 + 512], pb[bi][:, :], AF.Gelu,
                    bias=b1c[:, bcol:bcol + 1]), inc="act")
                release(key, "act", ac)
                gelu_last = ac
                if fcl == 7:
                    w1_rel[fb % 2] = P.cnt["pe"]
                    issue_w1()
            return gelu_last

        def out_phase(tt, e, out_rel, gelu_last):
            P.wait("pe", "act", gelu_last)
            for dh in range(2):
                for tsq in range(4):
                    ch, val = out_rel[tsq]
                    P.wait("pe", ch, val)
                for fk in range(FCN):
                    q = fk // 8
                    wch, dwc, sl = w2_at[(tt, e, dh, q)]
                    P.wait("pe", wch, dwc)
                    for tsq in range(4):
                        P.run("pe", lambda fk=fk, tsq=tsq, sl=sl: t.matmul(
                            pb[2 + tsq][:, :],
                            h1g[:, fk * 512 + tsq * 128:fk * 512 + tsq * 128 + 128],
                            w2sl[sl][:, (fk % 8) * 512:(fk % 8) * 512 + 512],
                            start=(fk == 0), stop=(fk == FCN - 1)),
                            inc="pe" if (fk % 8 == 7 and tsq == 3) else None)
                    if fk % 8 == 7:
                        _, _, sl_done = w2_at[(tt, e, dh, q)]
                        w2_rel[sl_done] = P.cnt["pe"]
                        issue_w2()
                pe_c = P.cnt["pe"]
                P.wait("dve", "pe", pe_c)
                for tsq in range(4):
                    gc = (tt * 4 + tsq) * 8 + e
                    dv = P.run("dve", lambda tsq=tsq, dh=dh, gc=gc: v.scalar_tensor_tensor(
                        acc[:, tsq * D + dh * 512:tsq * D + dh * 512 + 512],
                        pb[2 + tsq][:, :], gate_tm[:, gc:gc + 1],
                        acc[:, tsq * D + dh * 512:tsq * D + dh * 512 + 512],
                        ALU.mult, ALU.add), inc="dve")
                    out_rel[tsq] = ("dve", dv)
                    release(f"pb{2 + tsq}", "dve", dv)

        pending_ln2 = [None]
        for tt in range(NTT):
            if tt == 0:
                P.wait("pe", "dve", b_done)
            gelu_e0 = h1_phase(tt, 0, inject=pending_ln2[0])
            pending_ln2[0] = None
            if debug and tt == 0:
                P.wait("gp", "act", gelu_e0)
                _wch0, dwc0, sl0 = dw1_at[(0, 0)]
                P.run("gp", lambda sl0=sl0: g.dma_start(dbg["w1sl"], w1sl[sl0][:, :]),
                      inc="dbg", n=16)
                dxc = P.run("gp", lambda: g.dma_start(dbg["h1g"], h1g[:, :]), inc="dbg", n=16)
                P.wait("pe", "dbg", dxc)
            init_last = 0
            for ts in range(4):
                hsl = ts % 2
                gcol0 = (tt * 4 + ts) * 8
                dvg = P.run("dve", lambda gcol0=gcol0, ts=ts: v.tensor_copy(
                    rt_gb[:, ts * 8:ts * 8 + 8], gate_tm[:, gcol0:gcol0 + 8]), inc="dve")
                P.wait("pe", "dve", dvg)
                pe_c = pe_group("pb4", [lambda ts=ts: t.transpose(
                    pb[4][:, :].bitcast(BF16)[0:8, 0:128], rt_gb[:, ts * 8:ts * 8 + 8],
                    ident_bf[:, :])])
                P.wait("dve", "pe", pe_c)
                dvg = P.run("dve", lambda tt=tt, ts=ts: v.tensor_copy(
                    gate_fm[0:8, tt * TT + ts * 128:tt * TT + ts * 128 + 128],
                    pb[4][:, :].bitcast(BF16)[0:8, 0:128]), inc="dve")
                release("pb4", "dve", dvg)
                mms = [lambda c=c, tt=tt, ts=ts: t.transpose(
                    pkv6[:, :].bitcast(BF16)[:, c * 128:(c + 1) * 128],
                    h_fm[:, c * TOWN + tt * TT + ts * 128:c * TOWN + tt * TT + ts * 128 + 128],
                    ident_bf[:, :]) for c in range(DC)]
                pe_c = pe_group("pkv6", mms)
                P.wait("dve", "pe", pe_c)
                dvh = P.run("dve", lambda hsl=hsl: v.tensor_copy(
                    h_tm[hsl][:, :], pkv6[:, :].bitcast(BF16)[:, 0:1024]), inc="dve")
                release("pkv6", "dve", dvh)
                if debug and tt == 0 and ts == 0:
                    P.wait("gp", "dve", dvh)
                    P.run("gp", lambda: g.dma_start(dbg["hfm"], h_fm[:, :]), inc="dbg", n=16)
                    dxc = P.run("gp", lambda hsl=hsl: g.dma_start(dbg["htm"], h_tm[hsl][:, :]),
                                inc="dbg", n=16)
                    P.wait("dve", "dbg", dxc)
                P.wait("pe", "dve", dvg)
                for dh in range(2):
                    pe_c = pe_group("pkv7", [lambda tt=tt, ts=ts, dh=dh: t.matmul(
                        pkv7[:, :], gate_fm[0:8, tt * TT + ts * 128:tt * TT + ts * 128 + 128],
                        b2_sb[0:8, dh * 512:dh * 512 + 512], start=True, stop=True)])
                    P.wait("dve", "pe", pe_c)
                    dv2 = P.run("dve", lambda ts=ts, dh=dh, hsl=hsl: v.tensor_add(
                        acc[:, ts * D + dh * 512:ts * D + dh * 512 + 512],
                        h_tm[hsl][:, dh * 512:dh * 512 + 512], pkv7[:, :]), inc="dve")
                    release("pkv7", "dve", dv2)
                    init_last = dv2
            if debug and tt == 0:
                P.wait("gp", "dve", init_last)
                dxc = P.run("gp", lambda: g.dma_start(dbg["hin"], acc[:, :]), inc="dbg", n=16)
                P.wait("dve", "dbg", dxc)
            out_rel = {0: ("dve", init_last), 1: ("dve", init_last),
                       2: ("dve", init_last), 3: ("dve", init_last)}
            if cstub == "loads":
                for e in range(E):
                    load_w1(e, 0); load_w1(e, 1); load_w1(e, 2); load_w1(e, 3)
                    load_w2(e, 0); load_w2(e, 1); load_w2(e, 2); load_w2(e, 3)
                    w1_rel[0] = w1_rel[1] = P.cnt["pe"]
                    w2_rel[0] = w2_rel[1] = P.cnt["pe"]
            if cstub not in ("init", "loads"):
                out_phase(tt, 0, out_rel, gelu_e0)
                for e in range(1, E):
                    gl = h1_phase(tt, e)
                    out_phase(tt, e, out_rel, gl)
            if debug and tt == 0:
                P.wait("gp", "dve", out_rel[3][1])
                dxc = P.run("gp", lambda: g.dma_start(dbg["acc"], acc[:, :]), inc="dbg", n=16)
                P.wait("dve", "dbg", dxc)
            part2 = []
            for tsq in range(4):
                def mk_ln2(tt=tt, tsq=tsq):
                    def emitfn():
                        sl = tsq % 2
                        ob = (obuf0 if sl == 0 else obuf1)[:, :]
                        a_sl = acc[:, tsq * D:(tsq + 1) * D]
                        P.wait("dve", f"do{sl}", oslot_rel[sl])
                        P.run("dve", lambda: v.tensor_copy(ob, a_sl))
                        P.run("dve", lambda: v.scalar_tensor_tensor(
                            a_sl, ob, 1.0, ob, ALU.bypass, ALU.mult,
                            accum_out=ln2r[:, tsq * 4 + 1:tsq * 4 + 2]))
                        P.run("dve", lambda: v.tensor_scalar(
                            a_sl, ob, 0.0, None, ALU.add, ALU.add,
                            accum_out=ln2r[:, tsq * 4:tsq * 4 + 1]))
                        P.run("dve", lambda: v.tensor_scalar(
                            ln2r[:, tsq * 4:tsq * 4 + 1], ln2r[:, tsq * 4:tsq * 4 + 1],
                            1.0 / D, None, ALU.mult, ALU.bypass))
                        P.run("dve", lambda: v.tensor_scalar(
                            ln2r[:, tsq * 4 + 1:tsq * 4 + 2],
                            ln2r[:, tsq * 4 + 1:tsq * 4 + 2],
                            1.0 / D, None, ALU.mult, ALU.bypass))
                        P.run("dve", lambda: v.tensor_mul(
                            ln2r[:, tsq * 4 + 2:tsq * 4 + 3], ln2r[:, tsq * 4:tsq * 4 + 1],
                            ln2r[:, tsq * 4:tsq * 4 + 1]))
                        dv = P.run("dve", lambda: v.tensor_sub(
                            ln2r[:, tsq * 4 + 1:tsq * 4 + 2],
                            ln2r[:, tsq * 4 + 1:tsq * 4 + 2],
                            ln2r[:, tsq * 4 + 2:tsq * 4 + 3]), inc="dve")
                        P.wait("act", "dve", dv)
                        ac = P.run("act", lambda: s.activation(
                            ln2r[:, tsq * 4 + 2:tsq * 4 + 3],
                            ln2r[:, tsq * 4 + 1:tsq * 4 + 2],
                            AF.Sqrt, bias=eps_col[:, :]), inc="act")
                        P.wait("dve", "act", ac)
                        P.run("dve", lambda: v.reciprocal(
                            ln2r[:, tsq * 4 + 3:tsq * 4 + 4],
                            ln2r[:, tsq * 4 + 2:tsq * 4 + 3]))
                        dva = P.run("dve", lambda: v.tensor_scalar(
                            ob, ob, ln2r[:, tsq * 4:tsq * 4 + 1],
                            ln2r[:, tsq * 4 + 3:tsq * 4 + 4], ALU.subtract, ALU.mult),
                            inc="dve")
                        P.wait("sp", "dve", dva)
                        r0 = tt * TT + tsq * 128
                        oslot_rel[sl] = P.run("sp", lambda: sy.dma_start(
                            y[r0:r0 + 128, :], ob), inc=f"do{sl}", n=16)
                    return emitfn
                part2.append(mk_ln2())
            pending_ln2[0] = part2

        if pending_ln2[0]:
            for emitfn in pending_ln2[0]:
                emitfn()
            pending_ln2[0] = None

    ii = emit_init()
    kv_ready = emit_phase_a(ii)
    b_done = None
    if last_phase in ("B", "C"):
        b_done = emit_phase_b(ii, kv_ready)
    if last_phase == "C":
        if cstub == "dummy":
            P.wait("pe", "dve", b_done)
            for i in range(30000):
                P.run("pe", lambda: t.matmul(pb[0][:, :], ident_bf[:, :],
                                             wq_bf[:, 0:512], start=True, stop=True),
                      inc="pe" if i % 100 == 99 else None)
        elif cstub == "dummy2":
            pass
        else:
            emit_phase_c(b_done)
        emit_prepass(len(prepass_units))
    for tsq in range(4):
        P.wait("sp", f"do{tsq}", P.cnt[f"do{tsq}"])
    for ch in ("dxc", "dxa0", "dxa1", "dxb0", "dxb1", "dw1a", "dw1b", "dw2a",
               "dw2b", "dbg"):
        P.wait("sp", ch, P.cnt[ch])
    for e_ in range(E):
        P.wait("sp", f"pre{e_}", P.cnt[f"pre{e_}"])
    for eng_ in P.ENGS:
        P.flushw(eng_)

    with nc.Block() as block:
        @block.tensor
        def _(_):
            for fn in P.streams["pe"]:
                fn()

        @block.vector
        def _(_):
            for fn in P.streams["dve"]:
                fn()

        @block.scalar
        def _(_):
            for fn in P.streams["act"]:
                fn()

        @block.gpsimd
        def _(_):
            for fn in P.streams["gp"]:
                fn()

        @block.sync
        def _(_):
            for fn in P.streams["sp"]:
                fn()

    if cstub == "dummy2":
        with nc.Block() as block2:
            @block2.tensor
            def _(_):
                for i in range(30000):
                    t.matmul(pb[0][:, :], ident_bf[:, :], wq_bf[:, 0:512],
                             start=True, stop=True)

    return nc


_NC_CACHE = {}


def _get_nc(debug=False, last_phase="C"):
    key = (debug, last_phase)
    if key not in _NC_CACHE:
        _NC_CACHE[key] = build(debug=debug, last_phase=last_phase)
    return _NC_CACHE[key]


def make_inmaps(L0_attn_x, L0_attn_Wq, L0_attn_bq, L0_attn_Wk, L0_attn_bk,
                L0_attn_Wv, L0_attn_bv, L0_attn_Wo, L0_attn_bo,
                L0_ffn_Wr, L0_ffn_W1, L0_ffn_b1, L0_ffn_W2, L0_ffn_b2):
    c = np.ascontiguousarray
    f = lambda a: np.asarray(a, dtype=np.float32)
    shared = {
        "wqT": c(f(L0_attn_Wq).T), "wkT": c(f(L0_attn_Wk).T),
        "wvT": c(f(L0_attn_Wv).T), "woT": c(f(L0_attn_Wo).T),
        "bq": f(L0_attn_bq).reshape(1, D), "bk": f(L0_attn_bk).reshape(1, D),
        "bv": f(L0_attn_bv).reshape(1, D), "bo": f(L0_attn_bo).reshape(1, D),
        "wrT": c(f(L0_ffn_Wr).T),
        "w1": c(f(L0_ffn_W1)), "b1": c(f(L0_ffn_b1)),
        "w2": c(f(L0_ffn_W2)), "b2": c(f(L0_ffn_b2)),
    }
    in_maps = []
    for core in range(NCORE):
        b = core // 2
        h = core % 2
        xt = c(f(L0_attn_x)[b].T)
        m = dict(shared)
        m["xT"] = xt
        m["xTo"] = c(xt[:, h * TOWN:(h + 1) * TOWN])
        in_maps.append(m)
    return in_maps


def kernel(L0_attn_x, L0_attn_Wq, L0_attn_bq, L0_attn_Wk, L0_attn_bk,
           L0_attn_Wv, L0_attn_bv, L0_attn_Wo, L0_attn_bo,
           L0_ffn_Wr, L0_ffn_W1, L0_ffn_b1, L0_ffn_W2, L0_ffn_b2):
    in_maps = make_inmaps(L0_attn_x, L0_attn_Wq, L0_attn_bq, L0_attn_Wk, L0_attn_bk,
                          L0_attn_Wv, L0_attn_bv, L0_attn_Wo, L0_attn_bo,
                          L0_ffn_Wr, L0_ffn_W1, L0_ffn_b1, L0_ffn_W2, L0_ffn_b2)
    nc = _get_nc()
    res = run_bass_kernel_spmd(nc, in_maps, core_ids=list(range(NCORE)))
    out = np.empty((B, S, D), dtype=np.float32)
    for core in range(NCORE):
        b = core // 2
        h = core % 2
        out[b, h * TOWN:(h + 1) * TOWN, :] = res.results[core]["y"]
    return out


# version 07 (final)
# speedup vs baseline: 1.1850x; 1.1850x over previous
import numpy as np
import concourse.bass as bass
import concourse.mybir as mybir
from concourse.bass_utils import run_bass_kernel_spmd

F32 = mybir.dt.float32
BF16 = mybir.dt.bfloat16
AF = mybir.ActivationFunctionType
ALU = mybir.AluOpType
AX = mybir.AxisListType.X

NCORE = 8
B, S, D, E, F = 4, 8192, 1024, 8, 4096
DC = D // 128
FCN = F // 128
TOWN = 4096
TA = 256
NTA = S // TA
TB = 512
NTB = TOWN // TB
TT = 512
NTT = TOWN // TT
EPS = 1e-5
KB = 1024


class Prog:
    ENGS = ("pe", "dve", "act", "gp", "sp")
    CHANS = ("pe", "dve", "act", "gp", "dxc", "dxa0", "dxa1", "dxb0", "dxb1",
             "dw1a", "dw1b", "dw2a", "dw2b", "dw2c", "dw2d", "do0", "do1", "do2", "do3", "dbg",
             "pre0", "pre1", "pre2", "pre3", "pre4", "pre5", "pre6", "pre7")

    def __init__(self, nc):
        self.nc = nc
        self.streams = {e: [] for e in self.ENGS}
        self.sems = {c: nc.alloc_semaphore(c) for c in self.CHANS}
        self.cnt = {c: 0 for c in self.CHANS}
        self.lastw = {}
        self.pend = {}
        self.eng_obj = {"pe": nc.tensor, "dve": nc.vector, "act": nc.scalar,
                        "gp": nc.gpsimd, "sp": nc.sync}

    def emit(self, eng, fn):
        self.streams[eng].append(fn)

    def wait(self, eng, ch, val):
        if val is None or val <= 0:
            return
        key = (eng, ch)
        if self.lastw.get(key, 0) >= val:
            return
        self.lastw[key] = val
        self.pend.setdefault(eng, []).append((self.sems[ch], val))

    def flushw(self, eng):
        pend = self.pend.pop(eng, [])
        if not pend:
            return
        E_ = self.eng_obj[eng]
        for sem, val in pend:
            self.emit(eng, lambda E_=E_, sem=sem, val=val: E_.wait_ge(sem, val))

    def run(self, eng, fn, inc=None, n=1, dma=False):
        if eng in ("dve", "act") and not dma:
            self.wait(eng, eng, self.cnt[eng])
            if inc is None:
                inc = eng
        pend = self.pend.pop(eng, [])
        E_ = self.eng_obj[eng]
        if inc is None:
            sem = None
        else:
            sem = self.sems[inc]
            self.cnt[inc] += n

        def closure(fn=fn, sem=sem, n=n, pend=pend, E_=E_):
            for s_, v_ in pend[:-1]:
                E_.wait_ge(s_, v_)
            i = fn()
            if pend:
                i._wait_ge(pend[-1][0], pend[-1][1])
            if sem is not None:
                i.then_inc(sem, n)
        self.emit(eng, closure)
        return self.cnt[inc] if inc is not None else None


def build(debug=False, last_phase="C", cstub=None):
    nc = bass.Bass("TRN2", target_bir_lowering=False)
    t = nc.tensor
    v = nc.vector
    s = nc.scalar
    g = nc.gpsimd
    sy = nc.sync

    xT = nc.dram_tensor("xT", [D, S], F32, kind="ExternalInput").ap()
    xTo = nc.dram_tensor("xTo", [D, TOWN], F32, kind="ExternalInput").ap()
    wqT = nc.dram_tensor("wqT", [D, D], F32, kind="ExternalInput").ap()
    wkT = nc.dram_tensor("wkT", [D, D], F32, kind="ExternalInput").ap()
    wvT = nc.dram_tensor("wvT", [D, D], F32, kind="ExternalInput").ap()
    woT = nc.dram_tensor("woT", [D, D], F32, kind="ExternalInput").ap()
    bqh = nc.dram_tensor("bq", [1, D], F32, kind="ExternalInput").ap()
    bkh = nc.dram_tensor("bk", [1, D], F32, kind="ExternalInput").ap()
    bvh = nc.dram_tensor("bv", [1, D], F32, kind="ExternalInput").ap()
    boh = nc.dram_tensor("bo", [1, D], F32, kind="ExternalInput").ap()
    wrTh = nc.dram_tensor("wrT", [E, D], F32, kind="ExternalInput").ap()
    w1h = nc.dram_tensor("w1", [E, D, F], F32, kind="ExternalInput").ap()
    b1h = nc.dram_tensor("b1", [E, F], F32, kind="ExternalInput").ap()
    w2h = nc.dram_tensor("w2", [E, F, D], F32, kind="ExternalInput").ap()
    b2h = nc.dram_tensor("b2", [E, D], F32, kind="ExternalInput").ap()
    y = nc.dram_tensor("y", [TOWN, D], F32, kind="ExternalOutput").ap()
    w1s = nc.dram_tensor("w1s", [E, D, F], BF16, kind="Internal").ap()
    w2s = nc.dram_tensor("w2s", [E, F, D], BF16, kind="Internal").ap()

    dbg = {}
    if debug:
        dbg["kv"] = nc.dram_tensor("dbg_kv", [128, DC * D], F32, kind="ExternalOutput").ap()
        dbg["ksum"] = nc.dram_tensor("dbg_ksum", [128, 8], F32, kind="ExternalOutput").ap()
        if last_phase in ("B", "C"):
            dbg["h"] = nc.dram_tensor("dbg_h", [128, DC * TOWN], F32, kind="ExternalOutput").ap()
            dbg["gate"] = nc.dram_tensor("dbg_gate", [128, 256], F32, kind="ExternalOutput").ap()
        if last_phase == "C":
            dbg["hfm"] = nc.dram_tensor("dbg_hfm", [128, DC * TOWN], F32, kind="ExternalOutput").ap()
            dbg["htm"] = nc.dram_tensor("dbg_htm", [128, D], F32, kind="ExternalOutput").ap()
            dbg["hin"] = nc.dram_tensor("dbg_hin", [128, 4 * D], F32, kind="ExternalOutput").ap()
            dbg["acc"] = nc.dram_tensor("dbg_acc", [128, 4 * D], F32, kind="ExternalOutput").ap()
            dbg["h1g"] = nc.dram_tensor("dbg_h1g", [128, FCN * 512], F32, kind="ExternalOutput").ap()
            dbg["w1sl"] = nc.dram_tensor("dbg_w1sl", [128, 8 * KB], F32, kind="ExternalOutput").ap()

    SB0 = 16512
    def place(name, shape, dtype, off):
        return nc.alloc_sbuf_tensor_at(name, shape, dtype, offset=SB0 + off)

    off_a = 0

    def pa(name, shape, dtype):
        nonlocal off_a
        tsz = int(np.prod(shape[1:])) * mybir.dt.size(dtype)
        tns = place(name, shape, dtype, off_a)
        off_a += (tsz + 63) & ~63
        return tns

    ones_row = pa("ones_row", [1, 512], BF16)
    ones_col = pa("ones_col", [128, 1], BF16)
    ident_bf = pa("ident_bf", [128, 128], BF16)
    ident_f32 = pa("ident_f32", [128, 128], F32)
    bq_col = pa("bq_col", [128, 8], F32)
    bqn_col = pa("bqn_col", [128, 8], F32)
    bo_col = pa("bo_col", [128, 8], F32)
    wr_bf = pa("wr_bf", [128, 64], BF16)
    b2_sb = pa("b2_sb", [8, D], BF16)
    ksum_bf = pa("ksum_bf", [128, 8], BF16)
    b1c = pa("b1c", [128, 256], F32)
    gate_tm = pa("gate_tm", [128, 256], F32)
    gate_fm = pa("gate_fm", [8, TOWN], BF16)
    rt_nm = pa("rt_nm", [128, 8], F32)
    rt_ex = pa("rt_ex", [128, 32], F32)
    rt_g1 = pa("rt_g1", [128, 32], F32)
    rt_gb = pa("rt_gb", [128, 32], BF16)
    ln2r = pa("ln2r", [128, 16], F32)
    ksf = pa("ksf", [128, 8], F32)
    eps_col = pa("eps_col", [128, 1], F32)
    obuf1 = pa("obuf1", [128, D], F32)
    assert off_a <= 22528, off_a

    W0 = 22528
    wq_bf = place("wq_bf", [128, DC * D], BF16, W0)
    wo_bf = place("wo_bf", [128, DC * D], BF16, W0 + 16 * KB)
    wk_bf = place("wk_bf", [128, DC * D], BF16, W0 + 32 * KB)
    wv_bf = place("wv_bf", [128, DC * D], BF16, W0 + 48 * KB)
    kv_bf = place("kv_bf", [128, DC * D], BF16, W0 + 32 * KB)
    w1sl = [place(f"w1sl{i}", [128, 8 * KB], BF16, W0 + i * 16 * KB) for i in range(2)]
    w2sl = [place(f"w2sl{i}", [128, 4 * KB], BF16, W0 + 32 * KB + i * 8 * KB) for i in range(4)]

    H0 = 88064
    h_fm = place("h_fm", [128, DC * TOWN], BF16, H0)
    b1_nat = place("b1_nat", [8, F], F32, H0)
    bqrow = place("bqrow", [1, D], F32, H0 + 16 * KB)
    borow = place("borow", [1, D], F32, H0 + 20 * KB)
    wrT_bf = place("wrT_bf", [8, D], BF16, H0 + 24 * KB)
    bkrow = place("bkrow", [1, D], BF16, H0 + 26 * KB)
    bvrow = place("bvrow", [1, D], BF16, H0 + 28 * KB)

    D0 = 153600
    xbB = [place(f"xbB{i}", [128, DC * TB], BF16, D0 + i * 8 * KB) for i in range(2)]
    qs_bf = place("qs_bf", [128, DC * TB], BF16, D0 + 16 * KB)
    ctx_bf = place("ctx_bf", [128, DC * TB], BF16, D0 + 24 * KB)
    h1g = place("h1g", [128, FCN * 512], BF16, D0)

    E0 = 186368
    kv_acc = place("kv_acc", [128, DC * D], F32, D0)
    xbA = [place(f"xbA{i}", [128, DC * TA], BF16, E0 + i * 4 * KB) for i in range(2)]
    ktm = place("ktm", [128, 2 * D], BF16, E0 + 8 * KB)
    vtm = place("vtm", [128, 2 * D], BF16, E0 + 12 * KB)
    elu_a = place("elu_a", [128, 512], F32, E0 + 16 * KB)
    elu_m = place("elu_m", [128, 512], F32, E0 + 18 * KB)
    v_bf = place("v_bf", [128, DC * TB], BF16, E0)
    r_m = place("r_m", [1, 512], F32, E0 + 8 * KB)
    r_v = place("r_v", [1, 512], F32, E0 + 10 * KB)
    v2_bf = place("v2_bf", [128, 2 * 512], BF16, E0 + 12 * KB)
    r_t = place("r_t", [1, 512], F32, E0 + 14 * KB)
    arow_bf = place("arow_bf", [1, 512], BF16, E0 + 20 * KB)
    brow_bf = place("brow_bf", [1, 512], BF16, E0 + 21 * KB)
    zrow_bf = place("zrow_bf", [1, 512], BF16, E0 + 22 * KB)
    zb_sb = place("zb_sb", [128, 512], F32, E0 + 23 * KB)
    acc = place("acc", [128, 4 * D], F32, E0)
    h_tm = [place(f"h_tm{i}", [128, D], BF16, E0 + 16 * KB + i * 2 * KB) for i in range(2)]
    obuf0 = place("obuf0", [128, D], F32, E0 + 20 * KB)
    assert 16512 + E0 + 24 * KB + 64 <= 229344

    pb = [nc.alloc_psum_tensor(f"pb{i}", [128, 512], F32) for i in range(6)]
    pkv = nc.alloc_psum_tensor("pkv", [128, 1024], F32)
    pkv6 = pkv[:, 0:512]
    pkv7 = pkv[:, 512:1024]

    P = Prog(nc)
    bank_rel = {}

    def bank_wait(key):
        for ch, val in bank_rel.get(key, ()):
            P.wait("pe", ch, val)

    def release(key, ch, val, add=False):
        if add and key in bank_rel:
            bank_rel[key] = bank_rel[key] + [(ch, val)]
        else:
            bank_rel[key] = [(ch, val)]

    def pe_group(key, mms, inc=True):
        bank_wait(key)
        n = len(mms)
        for i, fn in enumerate(mms):
            P.run("pe", fn, inc="pe" if (i == n - 1 and inc) else None)
        return P.cnt["pe"]

    elu_guard = [0]

    xa_slot_rel = [0, 0]
    dx_xa = []

    def emit_load_xa(k):
        sl = k % 2
        P.wait("gp", "pe", xa_slot_rel[sl])
        t0 = k * TA
        ch = f"dxa{sl}"
        return ch, P.run("gp", lambda t0=t0, sl=sl: g.dma_start(
            xbA[sl][:, :].rearrange("p (c n) -> p c n", c=DC),
            xT[:, t0:t0 + TA].rearrange("(c p) n -> p c n", p=128)),
            inc=ch, n=16)

    def emit_init():
        gpv = P.run("gp", lambda: g.memset(ident_bf[:, :], 0.0), inc="gp")
        P.wait("gp", "gp", gpv)
        P.run("gp", lambda: g.affine_select(out=ident_bf[:, :], in_=ident_bf[:, :],
                                            compare_op=ALU.not_equal, fill=1.0, base=0,
                                            pattern=[[-1, 128]], channel_multiplier=1))
        gpv = P.run("gp", lambda: g.memset(ident_f32[:, :], 0.0), inc="gp")
        P.wait("gp", "gp", gpv)
        P.run("gp", lambda: g.affine_select(out=ident_f32[:, :], in_=ident_f32[:, :],
                                            compare_op=ALU.not_equal, fill=1.0, base=0,
                                            pattern=[[-1, 128]], channel_multiplier=1))
        P.run("gp", lambda: g.memset(ones_row[:, :], 1.0))
        P.run("gp", lambda: g.memset(ones_col[:, :], 1.0))
        gp_done = P.run("gp", lambda: g.memset(eps_col[:, :], EPS), inc="gp")

        def cast_w(dst, src):
            return P.run("gp", lambda dst=dst, src=src: g.dma_start(
                dst[:, :].rearrange("p (c o) -> p c o", c=DC),
                src.rearrange("(c p) o -> p c o", p=128)), inc="dxc", n=16)

        dx_wk = cast_w(wk_bf, wkT)
        dx_wv = cast_w(wv_bf, wvT)
        dx_xa.append(emit_load_xa(0))
        dx_wq = cast_w(wq_bf, wqT)
        dx_xa.append(emit_load_xa(1))
        dx_wo = cast_w(wo_bf, woT)
        P.run("gp", lambda: g.dma_start(bkrow[:, :], bkh), inc="dxc", n=16)
        P.run("gp", lambda: g.dma_start(bvrow[:, :], bvh), inc="dxc", n=16)
        P.run("gp", lambda: g.dma_start(bqrow[:, :], bqh), inc="dxc", n=16)
        P.run("gp", lambda: g.dma_start(borow[:, :], boh), inc="dxc", n=16)
        P.run("gp", lambda: g.dma_start(wrT_bf[:, :], wrTh), inc="dxc", n=16)
        P.run("gp", lambda: g.dma_start(b2_sb[:, :], b2h), inc="dxc", n=16)
        dx_rows = P.run("gp", lambda: g.dma_start(b1_nat[:, :], b1h), inc="dxc", n=16)

        P.wait("pe", "gp", gp_done)
        P.wait("pe", "dxc", dx_rows)
        for dq in range(DC):
            c1 = pe_group("pb0", [
                lambda dq=dq: t.transpose(pb[0][:, 0:1], bqrow[0:1, dq * 128:(dq + 1) * 128],
                                          ident_f32[0:1, 0:1]),
                lambda dq=dq: t.transpose(pb[0][:, 1:2], borow[0:1, dq * 128:(dq + 1) * 128],
                                          ident_f32[0:1, 0:1])])
            P.wait("dve", "pe", c1)
            P.run("dve", lambda dq=dq: v.tensor_copy(bq_col[:, dq:dq + 1], pb[0][:, 0:1]))
            dv = P.run("dve", lambda dq=dq: v.tensor_copy(bo_col[:, dq:dq + 1], pb[0][:, 1:2]),
                       inc="dve")
            release("pb0", "dve", dv)
        for dc in range(DC):
            c1 = pe_group("pb1", [
                lambda dc=dc: t.transpose(pb[1][:, :].bitcast(BF16)[:, 0:8],
                                          wrT_bf[0:8, dc * 128:(dc + 1) * 128],
                                          ident_bf[0:8, 0:8])])
            P.wait("dve", "pe", c1)
            dv = P.run("dve", lambda dc=dc: v.tensor_copy(
                wr_bf[:, dc * 8:(dc + 1) * 8], pb[1][:, :].bitcast(BF16)[:, 0:8]), inc="dve")
            release("pb1", "dve", dv)
        for fc in range(FCN):
            c1 = pe_group("pb2", [
                lambda fc=fc: t.transpose(pb[2][:, 0:8], b1_nat[0:8, fc * 128:(fc + 1) * 128],
                                          ident_f32[0:8, 0:8])])
            P.wait("dve", "pe", c1)
            dv = P.run("dve", lambda fc=fc: v.tensor_copy(
                b1c[:, fc * 8:(fc + 1) * 8], pb[2][:, 0:8]), inc="dve")
            release("pb2", "dve", dv)
        bqn_done = P.run("dve", lambda: v.tensor_scalar_mul(bqn_col[:, :], bq_col[:, :], -1.0),
                         inc="dve")
        return dict(dx_rows=dx_rows, bqn_done=bqn_done)

    prepass_units = []
    for e_ in range(E):
        for q_ in range(4):
            prepass_units.append(lambda e=e_, q=q_: g.dma_start(
                w1s[e, :, q * 1024:(q + 1) * 1024], w1h[e, :, q * 1024:(q + 1) * 1024]))
            prepass_units.append(lambda e=e_, q=q_: g.dma_start(
                w2s[e, q * 1024:(q + 1) * 1024, :], w2h[e, q * 1024:(q + 1) * 1024, :]))
    prepass_done_at = {}
    prepass_idx = [0]

    def emit_prepass(nmax):
        for _ in range(nmax):
            i = prepass_idx[0]
            if i >= len(prepass_units):
                return
            e = i // 8
            cnt = P.run("gp", prepass_units[i], inc=f"pre{e}", n=16)
            prepass_idx[0] += 1
            if i % 8 == 7:
                prepass_done_at[e] = cnt

    def emit_phase_a(ii):
        ks_started = [False]
        kv_slot_ctr = [0]
        kv_slots = [(pb[4][:, :], "pb4"), (pkv6[:, :], "pkv6"), (pkv7[:, :], "pkv7")]
        for k in range(NTA):
            sl = k % 2
            P.wait("pe", *dx_xa[k])
            if k == 0:
                P.wait("pe", "dxc", ii["dx_rows"])
            epil = []
            for ts in range(2):
                for proj in ("k", "v"):
                    wsb = wk_bf if proj == "k" else wv_bf
                    brw = bkrow if proj == "k" else bvrow
                    dst = ktm if proj == "k" else vtm
                    for nh in range(2):
                        bi = (0 if proj == "k" else 2) + nh
                        key = f"pb{bi}"
                        mms = [lambda dc=dc, nh=nh, wsb=wsb, bi=bi, ts=ts, sl=sl:
                               t.matmul(pb[bi][:, :],
                                        xbA[sl][:, dc * TA + ts * 128:dc * TA + ts * 128 + 128],
                                        wsb[:, dc * D + nh * 512:dc * D + nh * 512 + 512],
                                        start=(dc == 0), stop=False)
                               for dc in range(DC)]
                        mms.append(lambda nh=nh, brw=brw, bi=bi:
                                   t.matmul(pb[bi][:, :], ones_row[:, 0:128],
                                            brw[:, nh * 512:nh * 512 + 512],
                                            start=False, stop=True))
                        pe_c = pe_group(key, mms)
                        if proj == "v":
                            P.wait("dve", "pe", pe_c)
                            dv = P.run("dve", lambda bi=bi, ts=ts, nh=nh, dst=dst:
                                       v.tensor_copy(dst[:, ts * D + nh * 512:ts * D + nh * 512 + 512],
                                                     pb[bi][:, :]), inc="dve")
                            release(key, "dve", dv)
                            epil.append(dv)
                        else:
                            P.wait("act", "pe", pe_c)
                            P.wait("act", "dve", elu_guard[0])
                            P.run("act", lambda bi=bi: s.activation(elu_a[:, :], pb[bi][:, :],
                                                                    AF.Relu))
                            ac = P.run("act", lambda bi=bi: s.activation(
                                elu_m[:, :], pb[bi][:, :], AF.Relu, scale=-1.0), inc="act")
                            release(key, "act", ac)
                            ac = P.run("act", lambda: s.activation(
                                elu_m[:, :], elu_m[:, :], AF.Exp, scale=-1.0), inc="act")
                            P.wait("dve", "act", ac)
                            dv = P.run("dve", lambda ts=ts, nh=nh, dst=dst:
                                       v.tensor_add(dst[:, ts * D + nh * 512:ts * D + nh * 512 + 512],
                                                    elu_a[:, :], elu_m[:, :]), inc="dve")
                            elu_guard[0] = dv
                            epil.append(dv)
            for dvv in epil:
                P.wait("pe", "dve", dvv)
            for c in range(DC):
                for half in range(2):
                    si = kv_slot_ctr[0] % 3
                    kv_slot_ctr[0] += 1
                    bank_ap, key = kv_slots[si]
                    mms = [lambda c=c, half=half, ts=ts, bank_ap=bank_ap:
                           t.matmul(bank_ap,
                                    ktm[:, ts * D + c * 128:ts * D + c * 128 + 128],
                                    vtm[:, ts * D + half * 512:ts * D + half * 512 + 512],
                                    start=(ts == 0), stop=(ts == 1))
                           for ts in range(2)]
                    if half == 0:
                        for ts in range(2):
                            st = not ks_started[0]
                            ks_started[0] = True
                            last_ks = (k == NTA - 1 and c == DC - 1 and ts == 1)
                            mms.append(lambda c=c, ts=ts, st=st, last_ks=last_ks:
                                       t.matmul(pb[5][:, c:c + 1],
                                                ktm[:, ts * D + c * 128:ts * D + c * 128 + 128],
                                                ones_col[:, :], start=st, stop=last_ks))
                    pe_c = pe_group(key, mms)
                    P.wait("dve", "pe", pe_c)
                    dst = kv_acc[:, c * D + half * 512:c * D + half * 512 + 512]
                    if k == 0:
                        dv = P.run("dve", lambda dst=dst, bank_ap=bank_ap:
                                   v.tensor_copy(dst, bank_ap), inc="dve")
                    else:
                        dv = P.run("dve", lambda dst=dst, bank_ap=bank_ap:
                                   v.tensor_add(dst, dst, bank_ap), inc="dve")
                    release(key, "dve", dv)
            xa_slot_rel[sl] = P.cnt["pe"]
            if k + 2 < NTA:
                dx_xa.append(emit_load_xa(k + 2))
            emit_prepass(2)
        pe_c = P.cnt["pe"]
        P.wait("dve", "pe", pe_c)
        P.run("dve", lambda: v.tensor_scalar(ksf[:, :], pb[5][:, 0:8], 1e-6, None,
                                             ALU.add, ALU.bypass))
        dv = P.run("dve", lambda: v.tensor_copy(ksum_bf[:, :], ksf[:, :]), inc="dve")
        release("pb5", "dve", dv)
        for c in range(DC):
            dv = P.run("dve", lambda c=c: v.tensor_copy(
                kv_bf[:, c * D:(c + 1) * D], kv_acc[:, c * D:(c + 1) * D]), inc="dve")
        kv_ready = dv
        if debug:
            P.wait("gp", "dve", kv_ready)
            P.run("gp", lambda: g.dma_start(dbg["kv"], kv_acc[:, :]), inc="dbg", n=16)
            dbg_done = P.run("gp", lambda: g.dma_start(dbg["ksum"], ksf[:, :]), inc="dbg", n=16)
            dbg["a_done"] = dbg_done
        return kv_ready

    xb_rel = [0, 0]
    dx_xb = []

    def emit_load_xb(j):
        sl = j % 2
        P.wait("gp", "dve", xb_rel[sl])
        t0 = j * TB
        ch = f"dxb{sl}"
        return ch, P.run("gp", lambda t0=t0, sl=sl: g.dma_start(
            xbB[sl][:, :].rearrange("p (c n) -> p c n", c=DC),
            xTo[:, t0:t0 + TB].rearrange("(c p) n -> p c n", p=128)),
            inc=ch, n=16)

    def emit_phase_b(ii, kv_ready):
        P.wait("gp", "dve", kv_ready)
        dx_xb.append(emit_load_xb(0))
        dx_xb.append(emit_load_xb(1))
        P.wait("pe", "dve", kv_ready)
        P.wait("act", "dve", ii["bqn_done"])
        h_last_map = {}

        def emit_router(j, h_last):
                P.wait("pe", "dve", h_last)
                mms = []
                for tl in range(4):
                    tcol = j * TB + tl * 128
                    for c in range(DC):
                        mms.append(lambda c=c, tcol=tcol, tl=tl: t.matmul(
                            pb[5][:, tl * 8:tl * 8 + 8],
                            h_fm[:, c * TOWN + tcol:c * TOWN + tcol + 128],
                            wr_bf[:, c * 8:c * 8 + 8], start=(c == 0), stop=(c == DC - 1)))
                pe_c = pe_group("pb5", mms)
                P.wait("dve", "pe", pe_c)
                rel_list = []
                for tl in range(4):
                    dv = P.run("dve", lambda tl=tl: v.reduce_max(
                        rt_nm[:, tl:tl + 1], pb[5][:, tl * 8:tl * 8 + 8], axis=AX, negate=True),
                        inc="dve")
                    P.wait("act", "dve", dv)
                    ac = P.run("act", lambda tl=tl: s.activation(
                        rt_ex[:, tl * 8:tl * 8 + 8], pb[5][:, tl * 8:tl * 8 + 8], AF.Exp,
                        bias=rt_nm[:, tl:tl + 1]), inc="act")
                    P.wait("dve", "act", ac)
                    dv = P.run("dve", lambda tl=tl: v.reduce_sum(
                        rt_nm[:, 4 + tl:5 + tl], rt_ex[:, tl * 8:tl * 8 + 8], axis=AX), inc="dve")
                    dv = P.run("dve", lambda tl=tl: v.reciprocal(
                        rt_nm[:, 4 + tl:5 + tl], rt_nm[:, 4 + tl:5 + tl]), inc="dve")
                    dvq = P.run("dve", lambda tl=tl: v.tensor_scalar(
                        rt_g1[:, tl * 8:tl * 8 + 8], pb[5][:, tl * 8:tl * 8 + 8],
                        rt_nm[:, tl:tl + 1], 0.0, ALU.add, ALU.is_equal), inc="dve")
                    gcol0 = (j * 4 + tl) * 8
                    dv = P.run("dve", lambda gcol0=gcol0, tl=tl: v.tensor_scalar(
                        gate_tm[:, gcol0:gcol0 + 8], rt_g1[:, tl * 8:tl * 8 + 8],
                        rt_nm[:, 4 + tl:5 + tl], None, ALU.mult, ALU.bypass), inc="dve")
                    rel_list.append(("act", ac))
                    rel_list.append(("dve", dvq))
                bank_rel["pb5"] = rel_list[-2:] + [max(
                    (cv for cv in rel_list if cv[0] == "act"), key=lambda x: x[1]),
                    max((cv for cv in rel_list if cv[0] == "dve"), key=lambda x: x[1])]

        for j in range(NTB):
            sl = j % 2
            P.wait("pe", *dx_xb[j])
            q_last = 0
            for dq in range(DC):
                bi = dq % 2
                key = f"pb{bi}"
                mms = [lambda dc=dc, dq=dq, sl=sl, bi=bi:
                       t.matmul(pb[bi][:, :], wq_bf[:, dc * D + dq * 128:dc * D + dq * 128 + 128],
                                xbB[sl][:, dc * TB:dc * TB + 512],
                                start=(dc == 0), stop=(dc == DC - 1))
                       for dc in range(DC)]
                pe_c = pe_group(key, mms)
                P.wait("act", "pe", pe_c)
                P.wait("act", "dve", elu_guard[0])
                P.run("act", lambda bi=bi, dq=dq: s.activation(
                    elu_a[:, :], pb[bi][:, :], AF.Relu, bias=bq_col[:, dq:dq + 1]))
                ac = P.run("act", lambda bi=bi, dq=dq: s.activation(
                    elu_m[:, :], pb[bi][:, :], AF.Relu, scale=-1.0, bias=bqn_col[:, dq:dq + 1]),
                    inc="act")
                release(key, "act", ac)
                ac = P.run("act", lambda: s.activation(elu_m[:, :], elu_m[:, :], AF.Exp,
                                                       scale=-1.0), inc="act")
                P.wait("dve", "act", ac)
                dv = P.run("dve", lambda dq=dq: v.tensor_add(
                    qs_bf[:, dq * TB:dq * TB + 512], elu_a[:, :], elu_m[:, :]), inc="dve")
                elu_guard[0] = dv
                q_last = dv
            P.wait("pe", "dve", q_last)
            pe_c = pe_group("pkv6", [
                lambda c=c: t.matmul(pkv6[0:1, :], ksum_bf[:, c:c + 1],
                                     qs_bf[:, c * TB:c * TB + 512],
                                     start=(c == 0), stop=(c == DC - 1)) for c in range(DC)])
            P.wait("dve", "pe", pe_c)
            dv = P.run("dve", lambda: v.reciprocal(r_t[:, :], pkv6[0:1, :]), inc="dve")
            release("pkv6", "dve", dv)
            zr_done = P.run("dve", lambda: v.tensor_copy(zrow_bf[:, :], r_t[:, :]), inc="dve")
            zb_done = None
            ctx_last = 0
            for dvc in range(DC):
                bi = 2 + dvc % 2
                key = f"pb{bi}"
                mms = [lambda c=c, dvc=dvc, bi=bi:
                       t.matmul(pb[bi][:, :], kv_bf[:, c * D + dvc * 128:c * D + dvc * 128 + 128],
                                qs_bf[:, c * TB:c * TB + 512],
                                start=(c == 0), stop=(c == DC - 1)) for c in range(DC)]
                pe_c = pe_group(key, mms)
                if dvc == 0:
                    P.wait("pe", "dve", zr_done)
                    pe_z = pe_group("pkv7", [lambda: t.matmul(
                        pkv7[:, :], ones_row[:, 0:128], zrow_bf[:, :],
                        start=True, stop=True)])
                    P.wait("dve", "pe", pe_z)
                    zb_done = P.run("dve", lambda: v.tensor_copy(zb_sb[:, :], pkv7[:, :]),
                                    inc="dve")
                    release("pkv7", "dve", zb_done)
                P.wait("dve", "pe", pe_c)
                dv = P.run("dve", lambda bi=bi, dvc=dvc: v.tensor_copy(
                    ctx_bf[:, dvc * TB:dvc * TB + 512], pb[bi][:, :]), inc="dve")
                release(key, "dve", dv)
                ctx_last = dv
            P.wait("pe", "dve", ctx_last)
            v_last = 0
            for do in range(DC):
                bi = 4 + do % 2
                key = f"pb{bi}"
                mms = [lambda c=c, do=do, bi=bi:
                       t.matmul(pb[bi][:, :], wo_bf[:, c * D + do * 128:c * D + do * 128 + 128],
                                ctx_bf[:, c * TB:c * TB + 512],
                                start=(c == 0), stop=(c == DC - 1)) for c in range(DC)]
                pe_c = pe_group(key, mms)
                P.wait("dve", "pe", pe_c)
                P.wait("dve", "dve", elu_guard[0])
                P.run("dve", lambda bi=bi: v.tensor_mul(elu_a[:, :], pb[bi][:, :], zb_sb[:, :]))
                dv = P.run("dve", lambda do=do, sl=sl: v.scalar_tensor_tensor(
                    v_bf[:, do * TB:do * TB + 512], elu_a[:, :], bo_col[:, do:do + 1],
                    xbB[sl][:, do * TB:do * TB + 512], ALU.add, ALU.add), inc="dve")
                release(key, "dve", dv)
                v_last = dv
            xb_rel[sl] = v_last
            if j + 2 < NTB:
                dx_xb.append(emit_load_xb(j + 2))
            P.wait("pe", "dve", v_last)
            pe_s = pe_group("pkv6", [
                lambda c=c: t.matmul(pkv6[0:1, :], ones_col[:, :],
                                     v_bf[:, c * TB:c * TB + 512],
                                     start=(c == 0), stop=(c == DC - 1)) for c in range(DC)])
            bank_wait("pkv7")
            v2_pe = [0, 0]
            for c in range(DC):
                P.wait("dve", "pe", v2_pe[c % 2])
                dv = P.run("dve", lambda c=c: v.tensor_mul(
                    v2_bf[:, (c % 2) * 512:(c % 2) * 512 + 512],
                    v_bf[:, c * TB:c * TB + 512], v_bf[:, c * TB:c * TB + 512]), inc="dve")
                P.wait("pe", "dve", dv)
                pe_c = P.run("pe", lambda c=c: t.matmul(
                    pkv7[0:1, :], ones_col[:, :], v2_bf[:, (c % 2) * 512:(c % 2) * 512 + 512],
                    start=(c == 0), stop=(c == DC - 1)), inc="pe")
                v2_pe[c % 2] = pe_c
            pe_s2 = P.cnt["pe"]
            P.wait("dve", "pe", pe_s2)
            P.run("dve", lambda: v.tensor_scalar(r_m[:, :], pkv6[0:1, :], 1.0 / D, None,
                                                 ALU.mult, ALU.bypass))
            dv = P.run("dve", lambda: v.tensor_scalar(r_v[:, :], pkv7[0:1, :], 1.0 / D, None,
                                                      ALU.mult, ALU.bypass), inc="dve")
            release("pkv6", "dve", dv)
            release("pkv7", "dve", dv)
            P.wait("dve", "dve", dv)
            P.run("dve", lambda: v.tensor_mul(r_t[:, :], r_m[:, :], r_m[:, :]))
            dv = P.run("dve", lambda: v.tensor_sub(r_v[:, :], r_v[:, :], r_t[:, :]), inc="dve")
            P.wait("act", "dve", dv)
            P.run("act", lambda: s.activation(r_t[:, :], r_v[:, :], AF.Ln, bias=eps_col[0:1, :]))
            ac = P.run("act", lambda: s.activation(r_t[:, :], r_t[:, :], AF.Exp, scale=-0.5),
                       inc="act")
            P.wait("dve", "act", ac)
            dv = P.run("dve", lambda: v.tensor_copy(arow_bf[:, :], r_t[:, :]), inc="dve")
            P.wait("dve", "dve", dv)
            P.run("dve", lambda: v.tensor_mul(r_t[:, :], r_t[:, :], r_m[:, :]))
            dv2 = P.run("dve", lambda: v.tensor_scalar(brow_bf[:, :], r_t[:, :], -1.0, None,
                                                       ALU.mult, ALU.bypass), inc="dve")
            P.wait("pe", "dve", dv2)
            pe_group("pb2", [lambda: t.matmul(pb[2][:, :], ones_row[:, 0:128], arow_bf[:, :],
                                              start=True, stop=True)])
            pe_bb = pe_group("pb3", [lambda: t.matmul(pb[3][:, :], ones_row[:, 0:128],
                                                      brow_bf[:, :], start=True, stop=True)])
            P.wait("dve", "pe", pe_bb)
            P.wait("dve", "dve", elu_guard[0])
            h_last = 0
            for c in range(DC):
                P.run("dve", lambda c=c: v.tensor_mul(elu_a[:, :],
                                                      v_bf[:, c * TB:c * TB + 512], pb[2][:, :]))
                dv = P.run("dve", lambda c=c, j=j: v.tensor_add(
                    h_fm[:, c * TOWN + j * TB:c * TOWN + j * TB + 512], elu_a[:, :],
                    pb[3][:, :]), inc="dve")
                h_last = dv
            elu_guard[0] = h_last
            release("pb2", "dve", h_last)
            release("pb3", "dve", h_last)
            h_last_map[j] = h_last
            if j > 0:
                emit_router(j - 1, h_last_map[j - 1])
        emit_router(NTB - 1, h_last_map[NTB - 1])
        b_done = P.cnt["dve"]
        if debug:
            P.wait("gp", "dve", b_done)
            P.run("gp", lambda: g.dma_start(dbg["h"], h_fm[:, :]), inc="dbg", n=16)
            P.run("gp", lambda: g.dma_start(dbg["gate"], gate_tm[:, :]), inc="dbg", n=16)
        return b_done

    def emit_phase_c(b_done):
        pe_end_b = P.cnt["pe"]
        w1q = [(tt, e, fb) for tt in range(NTT) for e in range(E) for fb in range(4)]
        w2q = [(tt, e, dh, q) for tt in range(NTT) for e in range(E)
               for dh in range(2) for q in range(4)]
        w1_rel = [pe_end_b, pe_end_b]
        w2_rel = [pe_end_b] * 4
        w1_at = {}
        w2_at = {}
        w1_next = [0]
        w2_next = [0]

        def issue_w1():
            i = w1_next[0]
            if i >= len(w1q):
                return
            w1_next[0] += 1
            tt, e, fb = w1q[i]
            sl = i % 2
            ch = "dw1a" if sl == 0 else "dw1b"
            P.wait("act", "pe", w1_rel[sl])
            P.wait("act", f"pre{e}", prepass_done_at[e])
            c = P.run("act", lambda e=e, fb=fb, sl=sl: s.dma_start(
                w1sl[sl][:, :].rearrange("p (c f) -> p c f", c=DC),
                w1s[e, :, fb * 1024:(fb + 1) * 1024].rearrange("(c p) f -> p c f", p=128)),
                inc=ch, n=16, dma=True)
            w1_at[(tt, e, fb)] = (ch, c, sl)

        def issue_w2():
            i = w2_next[0]
            if i >= len(w2q):
                return
            w2_next[0] += 1
            tt, e, dh, q = w2q[i]
            sl = i % 4
            ch = "dw2" + "abcd"[sl]
            P.wait("sp", "pe", w2_rel[sl])
            P.wait("sp", f"pre{e}", prepass_done_at[e])
            c = P.run("sp", lambda e=e, dh=dh, q=q, sl=sl: sy.dma_start(
                w2sl[sl][:, :].rearrange("p (c d) -> p c d", c=8),
                w2s[e, q * 1024:(q + 1) * 1024, dh * 512:(dh + 1) * 512]
                .rearrange("(c p) d -> p c d", p=128)), inc=ch, n=16)
            w2_at[(tt, e, dh, q)] = (ch, c, sl)

        issue_w1()
        issue_w1()
        for _ in range(4):
            issue_w2()
        oslot_rel = [0, 0]

        def h1_phase(tt, e, inject=None):
            gelu_last = 0
            for fc in range(FCN):
                if inject and fc in (4, 12, 20, 28):
                    inject[(fc - 4) // 8]()
                fb = fc // 8
                fcl = fc % 8
                bi = fc % 2
                key = f"pb{bi}"
                wch, dwc, sl = w1_at[(tt, e, fb)]
                P.wait("pe", wch, dwc)
                mms = [lambda c=c, fcl=fcl, sl=sl, bi=bi, tt=tt: t.matmul(
                    pb[bi][:, :], w1sl[sl][:, c * 1024 + fcl * 128:c * 1024 + fcl * 128 + 128],
                    h_fm[:, c * TOWN + tt * TT:c * TOWN + tt * TT + 512],
                    start=(c == 0), stop=(c == DC - 1)) for c in range(DC)]
                pe_c = pe_group(key, mms)
                P.wait("act", "pe", pe_c)
                bcol = fc * 8 + e
                ac = P.run("act", lambda bi=bi, fc=fc, bcol=bcol: s.activation(
                    h1g[:, fc * 512:fc * 512 + 512], pb[bi][:, :], AF.Gelu,
                    bias=b1c[:, bcol:bcol + 1]), inc="act")
                release(key, "act", ac)
                gelu_last = ac
                if fcl == 7:
                    w1_rel[fb % 2] = P.cnt["pe"]
                    issue_w1()
            return gelu_last

        def out_phase(tt, e, out_rel, gelu_last):
            P.wait("pe", "act", gelu_last)
            for dh in range(2):
                for tsq in range(4):
                    ch, val = out_rel[tsq]
                    P.wait("pe", ch, val)
                for fk in range(FCN):
                    q = fk // 8
                    wch, dwc, sl = w2_at[(tt, e, dh, q)]
                    P.wait("pe", wch, dwc)
                    for tsq in range(4):
                        P.run("pe", lambda fk=fk, tsq=tsq, sl=sl: t.matmul(
                            pb[2 + tsq][:, :],
                            h1g[:, fk * 512 + tsq * 128:fk * 512 + tsq * 128 + 128],
                            w2sl[sl][:, (fk % 8) * 512:(fk % 8) * 512 + 512],
                            start=(fk == 0), stop=(fk == FCN - 1)),
                            inc="pe" if (fk % 8 == 7 and tsq == 3) else None)
                    if fk % 8 == 7:
                        _, _, sl_done = w2_at[(tt, e, dh, q)]
                        w2_rel[sl_done] = P.cnt["pe"]
                        issue_w2()
                pe_c = P.cnt["pe"]
                P.wait("dve", "pe", pe_c)
                for tsq in range(4):
                    gc = (tt * 4 + tsq) * 8 + e
                    dv = P.run("dve", lambda tsq=tsq, dh=dh, gc=gc: v.scalar_tensor_tensor(
                        acc[:, tsq * D + dh * 512:tsq * D + dh * 512 + 512],
                        pb[2 + tsq][:, :], gate_tm[:, gc:gc + 1],
                        acc[:, tsq * D + dh * 512:tsq * D + dh * 512 + 512],
                        ALU.mult, ALU.add), inc="dve")
                    out_rel[tsq] = ("dve", dv)
                    release(f"pb{2 + tsq}", "dve", dv)

        pending_ln2 = [None]
        for tt in range(NTT):
            if tt == 0:
                P.wait("pe", "dve", b_done)
            gelu_e0 = h1_phase(tt, 0, inject=pending_ln2[0])
            pending_ln2[0] = None
            if debug and tt == 0:
                P.wait("gp", "act", gelu_e0)
                _wch0, dwc0, sl0 = dw1_at[(0, 0)]
                P.run("gp", lambda sl0=sl0: g.dma_start(dbg["w1sl"], w1sl[sl0][:, :]),
                      inc="dbg", n=16)
                dxc = P.run("gp", lambda: g.dma_start(dbg["h1g"], h1g[:, :]), inc="dbg", n=16)
                P.wait("pe", "dbg", dxc)
            init_last = 0
            for ts in range(4):
                hsl = ts % 2
                gcol0 = (tt * 4 + ts) * 8
                dvg = P.run("dve", lambda gcol0=gcol0, ts=ts: v.tensor_copy(
                    rt_gb[:, ts * 8:ts * 8 + 8], gate_tm[:, gcol0:gcol0 + 8]), inc="dve")
                P.wait("pe", "dve", dvg)
                pe_c = pe_group("pb4", [lambda ts=ts: t.transpose(
                    pb[4][:, :].bitcast(BF16)[0:8, 0:128], rt_gb[:, ts * 8:ts * 8 + 8],
                    ident_bf[:, :])])
                P.wait("dve", "pe", pe_c)
                dvg = P.run("dve", lambda tt=tt, ts=ts: v.tensor_copy(
                    gate_fm[0:8, tt * TT + ts * 128:tt * TT + ts * 128 + 128],
                    pb[4][:, :].bitcast(BF16)[0:8, 0:128]), inc="dve")
                release("pb4", "dve", dvg)
                mms = [lambda c=c, tt=tt, ts=ts: t.transpose(
                    pkv6[:, :].bitcast(BF16)[:, c * 128:(c + 1) * 128],
                    h_fm[:, c * TOWN + tt * TT + ts * 128:c * TOWN + tt * TT + ts * 128 + 128],
                    ident_bf[:, :]) for c in range(DC)]
                pe_c = pe_group("pkv6", mms)
                P.wait("dve", "pe", pe_c)
                dvh = P.run("dve", lambda hsl=hsl: v.tensor_copy(
                    h_tm[hsl][:, :], pkv6[:, :].bitcast(BF16)[:, 0:1024]), inc="dve")
                release("pkv6", "dve", dvh)
                if debug and tt == 0 and ts == 0:
                    P.wait("gp", "dve", dvh)
                    P.run("gp", lambda: g.dma_start(dbg["hfm"], h_fm[:, :]), inc="dbg", n=16)
                    dxc = P.run("gp", lambda hsl=hsl: g.dma_start(dbg["htm"], h_tm[hsl][:, :]),
                                inc="dbg", n=16)
                    P.wait("dve", "dbg", dxc)
                P.wait("pe", "dve", dvg)
                for dh in range(2):
                    pe_c = pe_group("pkv7", [lambda tt=tt, ts=ts, dh=dh: t.matmul(
                        pkv7[:, :], gate_fm[0:8, tt * TT + ts * 128:tt * TT + ts * 128 + 128],
                        b2_sb[0:8, dh * 512:dh * 512 + 512], start=True, stop=True)])
                    P.wait("dve", "pe", pe_c)
                    dv2 = P.run("dve", lambda ts=ts, dh=dh, hsl=hsl: v.tensor_add(
                        acc[:, ts * D + dh * 512:ts * D + dh * 512 + 512],
                        h_tm[hsl][:, dh * 512:dh * 512 + 512], pkv7[:, :]), inc="dve")
                    release("pkv7", "dve", dv2)
                    init_last = dv2
            if debug and tt == 0:
                P.wait("gp", "dve", init_last)
                dxc = P.run("gp", lambda: g.dma_start(dbg["hin"], acc[:, :]), inc="dbg", n=16)
                P.wait("dve", "dbg", dxc)
            out_rel = {0: ("dve", init_last), 1: ("dve", init_last),
                       2: ("dve", init_last), 3: ("dve", init_last)}
            if cstub == "loads":
                for e in range(E):
                    load_w1(e, 0); load_w1(e, 1); load_w1(e, 2); load_w1(e, 3)
                    load_w2(e, 0); load_w2(e, 1); load_w2(e, 2); load_w2(e, 3)
                    w1_rel[0] = w1_rel[1] = P.cnt["pe"]
                    w2_rel[0] = w2_rel[1] = P.cnt["pe"]
            if cstub not in ("init", "loads"):
                out_phase(tt, 0, out_rel, gelu_e0)
                for e in range(1, E):
                    gl = h1_phase(tt, e)
                    out_phase(tt, e, out_rel, gl)
            if debug and tt == 0:
                P.wait("gp", "dve", out_rel[3][1])
                dxc = P.run("gp", lambda: g.dma_start(dbg["acc"], acc[:, :]), inc="dbg", n=16)
                P.wait("dve", "dbg", dxc)
            part2 = []
            for tsq in range(4):
                def mk_ln2(tt=tt, tsq=tsq):
                    def emitfn():
                        sl = tsq % 2
                        ob = (obuf0 if sl == 0 else obuf1)[:, :]
                        a_sl = acc[:, tsq * D:(tsq + 1) * D]
                        P.wait("dve", f"do{sl}", oslot_rel[sl])
                        P.run("dve", lambda: v.tensor_copy(ob, a_sl))
                        P.run("dve", lambda: v.scalar_tensor_tensor(
                            a_sl, ob, 1.0, ob, ALU.bypass, ALU.mult,
                            accum_out=ln2r[:, tsq * 4 + 1:tsq * 4 + 2]))
                        P.run("dve", lambda: v.tensor_scalar(
                            a_sl, ob, 0.0, None, ALU.add, ALU.add,
                            accum_out=ln2r[:, tsq * 4:tsq * 4 + 1]))
                        P.run("dve", lambda: v.tensor_scalar(
                            ln2r[:, tsq * 4:tsq * 4 + 1], ln2r[:, tsq * 4:tsq * 4 + 1],
                            1.0 / D, None, ALU.mult, ALU.bypass))
                        P.run("dve", lambda: v.tensor_scalar(
                            ln2r[:, tsq * 4 + 1:tsq * 4 + 2],
                            ln2r[:, tsq * 4 + 1:tsq * 4 + 2],
                            1.0 / D, None, ALU.mult, ALU.bypass))
                        P.run("dve", lambda: v.tensor_mul(
                            ln2r[:, tsq * 4 + 2:tsq * 4 + 3], ln2r[:, tsq * 4:tsq * 4 + 1],
                            ln2r[:, tsq * 4:tsq * 4 + 1]))
                        dv = P.run("dve", lambda: v.tensor_sub(
                            ln2r[:, tsq * 4 + 1:tsq * 4 + 2],
                            ln2r[:, tsq * 4 + 1:tsq * 4 + 2],
                            ln2r[:, tsq * 4 + 2:tsq * 4 + 3]), inc="dve")
                        P.wait("act", "dve", dv)
                        ac = P.run("act", lambda: s.activation(
                            ln2r[:, tsq * 4 + 2:tsq * 4 + 3],
                            ln2r[:, tsq * 4 + 1:tsq * 4 + 2],
                            AF.Sqrt, bias=eps_col[:, :]), inc="act")
                        P.wait("dve", "act", ac)
                        P.run("dve", lambda: v.reciprocal(
                            ln2r[:, tsq * 4 + 3:tsq * 4 + 4],
                            ln2r[:, tsq * 4 + 2:tsq * 4 + 3]))
                        dva = P.run("dve", lambda: v.tensor_scalar(
                            ob, ob, ln2r[:, tsq * 4:tsq * 4 + 1],
                            ln2r[:, tsq * 4 + 3:tsq * 4 + 4], ALU.subtract, ALU.mult),
                            inc="dve")
                        P.wait("sp", "dve", dva)
                        r0 = tt * TT + tsq * 128
                        oslot_rel[sl] = P.run("sp", lambda: sy.dma_start(
                            y[r0:r0 + 128, :], ob), inc=f"do{sl}", n=16)
                    return emitfn
                part2.append(mk_ln2())
            pending_ln2[0] = part2

        if pending_ln2[0]:
            for emitfn in pending_ln2[0]:
                emitfn()
            pending_ln2[0] = None

    ii = emit_init()
    kv_ready = emit_phase_a(ii)
    b_done = None
    if last_phase in ("B", "C"):
        b_done = emit_phase_b(ii, kv_ready)
    if last_phase == "C":
        if cstub == "dummy":
            P.wait("pe", "dve", b_done)
            for i in range(30000):
                P.run("pe", lambda: t.matmul(pb[0][:, :], ident_bf[:, :],
                                             wq_bf[:, 0:512], start=True, stop=True),
                      inc="pe" if i % 100 == 99 else None)
        elif cstub == "dummy2":
            pass
        else:
            emit_phase_c(b_done)
        emit_prepass(len(prepass_units))
    for tsq in range(4):
        P.wait("sp", f"do{tsq}", P.cnt[f"do{tsq}"])
    for ch in ("dxc", "dxa0", "dxa1", "dxb0", "dxb1", "dw1a", "dw1b", "dw2a",
               "dw2b", "dbg"):
        P.wait("sp", ch, P.cnt[ch])
    for e_ in range(E):
        P.wait("sp", f"pre{e_}", P.cnt[f"pre{e_}"])
    for eng_ in P.ENGS:
        P.flushw(eng_)

    with nc.Block() as block:
        @block.tensor
        def _(_):
            for fn in P.streams["pe"]:
                fn()

        @block.vector
        def _(_):
            for fn in P.streams["dve"]:
                fn()

        @block.scalar
        def _(_):
            for fn in P.streams["act"]:
                fn()

        @block.gpsimd
        def _(_):
            for fn in P.streams["gp"]:
                fn()

        @block.sync
        def _(_):
            for fn in P.streams["sp"]:
                fn()

    if cstub == "dummy2":
        with nc.Block() as block2:
            @block2.tensor
            def _(_):
                for i in range(30000):
                    t.matmul(pb[0][:, :], ident_bf[:, :], wq_bf[:, 0:512],
                             start=True, stop=True)

    return nc


_NC_CACHE = {}


def _get_nc(debug=False, last_phase="C"):
    key = (debug, last_phase)
    if key not in _NC_CACHE:
        _NC_CACHE[key] = build(debug=debug, last_phase=last_phase)
    return _NC_CACHE[key]


def make_inmaps(L0_attn_x, L0_attn_Wq, L0_attn_bq, L0_attn_Wk, L0_attn_bk,
                L0_attn_Wv, L0_attn_bv, L0_attn_Wo, L0_attn_bo,
                L0_ffn_Wr, L0_ffn_W1, L0_ffn_b1, L0_ffn_W2, L0_ffn_b2):
    c = np.ascontiguousarray
    f = lambda a: np.asarray(a, dtype=np.float32)
    shared = {
        "wqT": c(f(L0_attn_Wq).T), "wkT": c(f(L0_attn_Wk).T),
        "wvT": c(f(L0_attn_Wv).T), "woT": c(f(L0_attn_Wo).T),
        "bq": f(L0_attn_bq).reshape(1, D), "bk": f(L0_attn_bk).reshape(1, D),
        "bv": f(L0_attn_bv).reshape(1, D), "bo": f(L0_attn_bo).reshape(1, D),
        "wrT": c(f(L0_ffn_Wr).T),
        "w1": c(f(L0_ffn_W1)), "b1": c(f(L0_ffn_b1)),
        "w2": c(f(L0_ffn_W2)), "b2": c(f(L0_ffn_b2)),
    }
    in_maps = []
    for core in range(NCORE):
        b = core // 2
        h = core % 2
        xt = c(f(L0_attn_x)[b].T)
        m = dict(shared)
        m["xT"] = xt
        m["xTo"] = c(xt[:, h * TOWN:(h + 1) * TOWN])
        in_maps.append(m)
    return in_maps


def kernel(L0_attn_x, L0_attn_Wq, L0_attn_bq, L0_attn_Wk, L0_attn_bk,
           L0_attn_Wv, L0_attn_bv, L0_attn_Wo, L0_attn_bo,
           L0_ffn_Wr, L0_ffn_W1, L0_ffn_b1, L0_ffn_W2, L0_ffn_b2):
    in_maps = make_inmaps(L0_attn_x, L0_attn_Wq, L0_attn_bq, L0_attn_Wk, L0_attn_bk,
                          L0_attn_Wv, L0_attn_bv, L0_attn_Wo, L0_attn_bo,
                          L0_ffn_Wr, L0_ffn_W1, L0_ffn_b1, L0_ffn_W2, L0_ffn_b2)
    nc = _get_nc()
    res = run_bass_kernel_spmd(nc, in_maps, core_ids=list(range(NCORE)))
    out = np.empty((B, S, D), dtype=np.float32)
    for core in range(NCORE):
        b = core // 2
        h = core % 2
        out[b, h * TOWN:(h + 1) * TOWN, :] = res.results[core]["y"]
    return out


# version 08
# speedup vs baseline: 1.1982x; 1.0111x over previous
import numpy as np
import concourse.bass as bass
import concourse.mybir as mybir
from concourse.bass_utils import run_bass_kernel_spmd

F32 = mybir.dt.float32
BF16 = mybir.dt.bfloat16
AF = mybir.ActivationFunctionType
ALU = mybir.AluOpType
AX = mybir.AxisListType.X

NCORE = 8
B, S, D, E, F = 4, 8192, 1024, 8, 4096
DC = D // 128
FCN = F // 128
TOWN = 4096
TA = 256
NTA = S // TA
TB = 512
NTB = TOWN // TB
TT = 512
NTT = TOWN // TT
EPS = 1e-5
KB = 1024


class Prog:
    ENGS = ("pe", "dve", "act", "gp", "sp")
    CHANS = ("pe", "dve", "act", "gp", "dxc", "dxa0", "dxa1", "dxb0", "dxb1",
             "dw1a", "dw1b", "dw2a", "dw2b", "dw2c", "dw2d", "do0", "do1", "do2", "do3", "dbg",
             "pre0", "pre1", "pre2", "pre3", "pre4", "pre5", "pre6", "pre7")

    def __init__(self, nc):
        self.nc = nc
        self.streams = {e: [] for e in self.ENGS}
        self.sems = {c: nc.alloc_semaphore(c) for c in self.CHANS}
        self.cnt = {c: 0 for c in self.CHANS}
        self.lastw = {}
        self.pend = {}
        self.eng_obj = {"pe": nc.tensor, "dve": nc.vector, "act": nc.scalar,
                        "gp": nc.gpsimd, "sp": nc.sync}

    def emit(self, eng, fn):
        self.streams[eng].append(fn)

    def wait(self, eng, ch, val):
        if val is None or val <= 0:
            return
        key = (eng, ch)
        if self.lastw.get(key, 0) >= val:
            return
        self.lastw[key] = val
        self.pend.setdefault(eng, []).append((self.sems[ch], val))

    def flushw(self, eng):
        pend = self.pend.pop(eng, [])
        if not pend:
            return
        E_ = self.eng_obj[eng]
        for sem, val in pend:
            self.emit(eng, lambda E_=E_, sem=sem, val=val: E_.wait_ge(sem, val))

    def run(self, eng, fn, inc=None, n=1, dma=False):
        if eng in ("dve", "act") and not dma:
            self.wait(eng, eng, self.cnt[eng])
            if inc is None:
                inc = eng
        pend = self.pend.pop(eng, [])
        E_ = self.eng_obj[eng]
        if inc is None:
            sem = None
        else:
            sem = self.sems[inc]
            self.cnt[inc] += n

        def closure(fn=fn, sem=sem, n=n, pend=pend, E_=E_):
            for s_, v_ in pend[:-1]:
                E_.wait_ge(s_, v_)
            i = fn()
            if pend:
                i._wait_ge(pend[-1][0], pend[-1][1])
            if sem is not None:
                i.then_inc(sem, n)
        self.emit(eng, closure)
        return self.cnt[inc] if inc is not None else None


def build(debug=False, last_phase="C", cstub=None):
    nc = bass.Bass("TRN2", target_bir_lowering=False)
    t = nc.tensor
    v = nc.vector
    s = nc.scalar
    g = nc.gpsimd
    sy = nc.sync

    xT = nc.dram_tensor("xT", [D, S], F32, kind="ExternalInput").ap()
    xTo = nc.dram_tensor("xTo", [D, TOWN], F32, kind="ExternalInput").ap()
    wqT = nc.dram_tensor("wqT", [D, D], F32, kind="ExternalInput").ap()
    wkT = nc.dram_tensor("wkT", [D, D], F32, kind="ExternalInput").ap()
    wvT = nc.dram_tensor("wvT", [D, D], F32, kind="ExternalInput").ap()
    woT = nc.dram_tensor("woT", [D, D], F32, kind="ExternalInput").ap()
    bqh = nc.dram_tensor("bq", [1, D], F32, kind="ExternalInput").ap()
    bkh = nc.dram_tensor("bk", [1, D], F32, kind="ExternalInput").ap()
    bvh = nc.dram_tensor("bv", [1, D], F32, kind="ExternalInput").ap()
    boh = nc.dram_tensor("bo", [1, D], F32, kind="ExternalInput").ap()
    wrTh = nc.dram_tensor("wrT", [E, D], F32, kind="ExternalInput").ap()
    w1h = nc.dram_tensor("w1", [E, D, F], F32, kind="ExternalInput").ap()
    b1h = nc.dram_tensor("b1", [E, F], F32, kind="ExternalInput").ap()
    w2h = nc.dram_tensor("w2", [E, F, D], F32, kind="ExternalInput").ap()
    b2h = nc.dram_tensor("b2", [E, D], F32, kind="ExternalInput").ap()
    y = nc.dram_tensor("y", [TOWN, D], F32, kind="ExternalOutput").ap()
    w1s = nc.dram_tensor("w1s", [E, D, F], BF16, kind="Internal").ap()
    w2s = nc.dram_tensor("w2s", [E, F, D], BF16, kind="Internal").ap()

    dbg = {}
    if debug:
        dbg["kv"] = nc.dram_tensor("dbg_kv", [128, DC * D], F32, kind="ExternalOutput").ap()
        dbg["ksum"] = nc.dram_tensor("dbg_ksum", [128, 8], F32, kind="ExternalOutput").ap()
        if last_phase in ("B", "C"):
            dbg["h"] = nc.dram_tensor("dbg_h", [128, DC * TOWN], F32, kind="ExternalOutput").ap()
            dbg["gate"] = nc.dram_tensor("dbg_gate", [128, 256], F32, kind="ExternalOutput").ap()
        if last_phase == "C":
            dbg["hfm"] = nc.dram_tensor("dbg_hfm", [128, DC * TOWN], F32, kind="ExternalOutput").ap()
            dbg["htm"] = nc.dram_tensor("dbg_htm", [128, D], F32, kind="ExternalOutput").ap()
            dbg["hin"] = nc.dram_tensor("dbg_hin", [128, 4 * D], F32, kind="ExternalOutput").ap()
            dbg["acc"] = nc.dram_tensor("dbg_acc", [128, 4 * D], F32, kind="ExternalOutput").ap()
            dbg["h1g"] = nc.dram_tensor("dbg_h1g", [128, FCN * 512], F32, kind="ExternalOutput").ap()
            dbg["w1sl"] = nc.dram_tensor("dbg_w1sl", [128, 8 * KB], F32, kind="ExternalOutput").ap()

    SB0 = 16512
    def place(name, shape, dtype, off):
        return nc.alloc_sbuf_tensor_at(name, shape, dtype, offset=SB0 + off)

    off_a = 0

    def pa(name, shape, dtype):
        nonlocal off_a
        tsz = int(np.prod(shape[1:])) * mybir.dt.size(dtype)
        tns = place(name, shape, dtype, off_a)
        off_a += (tsz + 63) & ~63
        return tns

    ones_row = pa("ones_row", [1, 512], BF16)
    ones_col = pa("ones_col", [128, 1], BF16)
    ident_bf = pa("ident_bf", [128, 128], BF16)
    ident_f32 = pa("ident_f32", [128, 128], F32)
    bq_col = pa("bq_col", [128, 8], F32)
    bqn_col = pa("bqn_col", [128, 8], F32)
    bo_col = pa("bo_col", [128, 8], F32)
    wr_bf = pa("wr_bf", [128, 64], BF16)
    b2_sb = pa("b2_sb", [8, D], BF16)
    ksum_bf = pa("ksum_bf", [128, 8], BF16)
    b1c = pa("b1c", [128, 256], F32)
    gate_tm = pa("gate_tm", [128, 256], F32)
    gate_fm = pa("gate_fm", [8, TOWN], BF16)
    rt_nm = pa("rt_nm", [128, 8], F32)
    rt_ex = pa("rt_ex", [128, 32], F32)
    rt_g1 = pa("rt_g1", [128, 32], F32)
    rt_gb = pa("rt_gb", [128, 32], BF16)
    ln2r = pa("ln2r", [128, 16], F32)
    ksf = pa("ksf", [128, 8], F32)
    eps_col = pa("eps_col", [128, 1], F32)
    obuf1 = pa("obuf1", [128, D], F32)
    r_t = pa("r_t", [1, 512], F32)
    assert off_a <= 22528, off_a

    W0 = 22528
    wq_bf = place("wq_bf", [128, DC * D], BF16, W0)
    wo_bf = place("wo_bf", [128, DC * D], BF16, W0 + 16 * KB)
    wk_bf = place("wk_bf", [128, DC * D], BF16, W0 + 32 * KB)
    wv_bf = place("wv_bf", [128, DC * D], BF16, W0 + 48 * KB)
    kv_bf = place("kv_bf", [128, DC * D], BF16, W0 + 32 * KB)
    w1sl = [place(f"w1sl{i}", [128, 8 * KB], BF16, W0 + i * 16 * KB) for i in range(2)]
    w2sl = [place(f"w2sl{i}", [128, 4 * KB], BF16, W0 + 32 * KB + i * 8 * KB) for i in range(4)]

    H0 = 88064
    h_fm = place("h_fm", [128, DC * TOWN], BF16, H0)
    b1_nat = place("b1_nat", [8, F], F32, H0)
    bqrow = place("bqrow", [1, D], F32, H0 + 16 * KB)
    borow = place("borow", [1, D], F32, H0 + 20 * KB)
    wrT_bf = place("wrT_bf", [8, D], BF16, H0 + 24 * KB)
    bkrow = place("bkrow", [1, D], BF16, H0 + 26 * KB)
    bvrow = place("bvrow", [1, D], BF16, H0 + 28 * KB)

    D0 = 153600
    xbB = [place(f"xbB{i}", [128, DC * TB], BF16, D0 + i * 8 * KB) for i in range(2)]
    qs_bf = place("qs_bf", [128, DC * TB], BF16, D0 + 16 * KB)
    ctx_bf = place("ctx_bf", [128, DC * TB], BF16, D0 + 24 * KB)
    h1g = place("h1g", [128, FCN * 512], BF16, D0)

    E0 = 186368
    kv_acc = place("kv_acc", [128, DC * D], F32, D0)
    xbA = [place(f"xbA{i}", [128, DC * TA], BF16, E0 + i * 4 * KB) for i in range(2)]
    ktm = place("ktm", [128, 2 * D], BF16, E0 + 8 * KB)
    vtm = place("vtm", [128, 2 * D], BF16, E0 + 12 * KB)
    elu_a = place("elu_a", [128, 512], F32, E0 + 16 * KB)
    elu_m = place("elu_m", [128, 512], F32, E0 + 18 * KB)
    v_bf = place("v_bf", [128, DC * TB], BF16, E0)
    r_m = place("r_m", [1, 512], F32, E0 + 8 * KB)
    r_v = place("r_v", [1, 512], F32, E0 + 10 * KB)
    v2_bf = place("v2_bf", [128, 4 * 512], BF16, E0 + 12 * KB)

    arow_bf = place("arow_bf", [1, 512], BF16, E0 + 20 * KB)
    brow_bf = place("brow_bf", [1, 512], BF16, E0 + 21 * KB)
    zrow_bf = place("zrow_bf", [1, 512], BF16, E0 + 22 * KB)
    zb_sb = place("zb_sb", [128, 512], F32, E0 + 23 * KB)
    acc = place("acc", [128, 4 * D], F32, E0)
    h_tm = [place(f"h_tm{i}", [128, D], BF16, E0 + 16 * KB + i * 2 * KB) for i in range(2)]
    obuf0 = place("obuf0", [128, D], F32, E0 + 20 * KB)
    assert 16512 + E0 + 24 * KB + 64 <= 229344

    pb = [nc.alloc_psum_tensor(f"pb{i}", [128, 512], F32) for i in range(6)]
    pkv = nc.alloc_psum_tensor("pkv", [128, 1024], F32)
    pkv6 = pkv[:, 0:512]
    pkv7 = pkv[:, 512:1024]

    P = Prog(nc)
    bank_rel = {}

    def bank_wait(key):
        for ch, val in bank_rel.get(key, ()):
            P.wait("pe", ch, val)

    def release(key, ch, val, add=False):
        if add and key in bank_rel:
            bank_rel[key] = bank_rel[key] + [(ch, val)]
        else:
            bank_rel[key] = [(ch, val)]

    def pe_group(key, mms, inc=True):
        bank_wait(key)
        n = len(mms)
        for i, fn in enumerate(mms):
            P.run("pe", fn, inc="pe" if (i == n - 1 and inc) else None)
        return P.cnt["pe"]

    elu_guard = [0]

    xa_slot_rel = [0, 0]
    dx_xa = []

    def emit_load_xa(k):
        sl = k % 2
        P.wait("gp", "pe", xa_slot_rel[sl])
        t0 = k * TA
        ch = f"dxa{sl}"
        return ch, P.run("gp", lambda t0=t0, sl=sl: g.dma_start(
            xbA[sl][:, :].rearrange("p (c n) -> p c n", c=DC),
            xT[:, t0:t0 + TA].rearrange("(c p) n -> p c n", p=128)),
            inc=ch, n=16)

    def emit_init():
        gpv = P.run("gp", lambda: g.memset(ident_bf[:, :], 0.0), inc="gp")
        P.wait("gp", "gp", gpv)
        P.run("gp", lambda: g.affine_select(out=ident_bf[:, :], in_=ident_bf[:, :],
                                            compare_op=ALU.not_equal, fill=1.0, base=0,
                                            pattern=[[-1, 128]], channel_multiplier=1))
        gpv = P.run("gp", lambda: g.memset(ident_f32[:, :], 0.0), inc="gp")
        P.wait("gp", "gp", gpv)
        P.run("gp", lambda: g.affine_select(out=ident_f32[:, :], in_=ident_f32[:, :],
                                            compare_op=ALU.not_equal, fill=1.0, base=0,
                                            pattern=[[-1, 128]], channel_multiplier=1))
        P.run("gp", lambda: g.memset(ones_row[:, :], 1.0))
        P.run("gp", lambda: g.memset(ones_col[:, :], 1.0))
        gp_done = P.run("gp", lambda: g.memset(eps_col[:, :], EPS), inc="gp")

        def cast_w(dst, src):
            return P.run("gp", lambda dst=dst, src=src: g.dma_start(
                dst[:, :].rearrange("p (c o) -> p c o", c=DC),
                src.rearrange("(c p) o -> p c o", p=128)), inc="dxc", n=16)

        dx_wk = cast_w(wk_bf, wkT)
        dx_wv = cast_w(wv_bf, wvT)
        dx_xa.append(emit_load_xa(0))
        dx_wq = cast_w(wq_bf, wqT)
        dx_xa.append(emit_load_xa(1))
        dx_wo = cast_w(wo_bf, woT)
        P.run("gp", lambda: g.dma_start(bkrow[:, :], bkh), inc="dxc", n=16)
        P.run("gp", lambda: g.dma_start(bvrow[:, :], bvh), inc="dxc", n=16)
        P.run("gp", lambda: g.dma_start(bqrow[:, :], bqh), inc="dxc", n=16)
        P.run("gp", lambda: g.dma_start(borow[:, :], boh), inc="dxc", n=16)
        P.run("gp", lambda: g.dma_start(wrT_bf[:, :], wrTh), inc="dxc", n=16)
        P.run("gp", lambda: g.dma_start(b2_sb[:, :], b2h), inc="dxc", n=16)
        dx_rows = P.run("gp", lambda: g.dma_start(b1_nat[:, :], b1h), inc="dxc", n=16)

        P.wait("pe", "gp", gp_done)
        P.wait("pe", "dxc", dx_rows)
        for dq in range(DC):
            c1 = pe_group("pb0", [
                lambda dq=dq: t.transpose(pb[0][:, 0:1], bqrow[0:1, dq * 128:(dq + 1) * 128],
                                          ident_f32[0:1, 0:1]),
                lambda dq=dq: t.transpose(pb[0][:, 1:2], borow[0:1, dq * 128:(dq + 1) * 128],
                                          ident_f32[0:1, 0:1])])
            P.wait("dve", "pe", c1)
            P.run("dve", lambda dq=dq: v.tensor_copy(bq_col[:, dq:dq + 1], pb[0][:, 0:1]))
            dv = P.run("dve", lambda dq=dq: v.tensor_copy(bo_col[:, dq:dq + 1], pb[0][:, 1:2]),
                       inc="dve")
            release("pb0", "dve", dv)
        for dc in range(DC):
            c1 = pe_group("pb1", [
                lambda dc=dc: t.transpose(pb[1][:, :].bitcast(BF16)[:, 0:8],
                                          wrT_bf[0:8, dc * 128:(dc + 1) * 128],
                                          ident_bf[0:8, 0:8])])
            P.wait("dve", "pe", c1)
            dv = P.run("dve", lambda dc=dc: v.tensor_copy(
                wr_bf[:, dc * 8:(dc + 1) * 8], pb[1][:, :].bitcast(BF16)[:, 0:8]), inc="dve")
            release("pb1", "dve", dv)
        for fc in range(FCN):
            c1 = pe_group("pb2", [
                lambda fc=fc: t.transpose(pb[2][:, 0:8], b1_nat[0:8, fc * 128:(fc + 1) * 128],
                                          ident_f32[0:8, 0:8])])
            P.wait("dve", "pe", c1)
            dv = P.run("dve", lambda fc=fc: v.tensor_copy(
                b1c[:, fc * 8:(fc + 1) * 8], pb[2][:, 0:8]), inc="dve")
            release("pb2", "dve", dv)
        bqn_done = P.run("dve", lambda: v.tensor_scalar_mul(bqn_col[:, :], bq_col[:, :], -1.0),
                         inc="dve")
        return dict(dx_rows=dx_rows, bqn_done=bqn_done)

    prepass_units = []
    for e_ in range(E):
        for q_ in range(4):
            prepass_units.append(lambda e=e_, q=q_: g.dma_start(
                w1s[e, :, q * 1024:(q + 1) * 1024], w1h[e, :, q * 1024:(q + 1) * 1024]))
            prepass_units.append(lambda e=e_, q=q_: g.dma_start(
                w2s[e, q * 1024:(q + 1) * 1024, :], w2h[e, q * 1024:(q + 1) * 1024, :]))
    prepass_done_at = {}
    prepass_idx = [0]

    def emit_prepass(nmax):
        for _ in range(nmax):
            i = prepass_idx[0]
            if i >= len(prepass_units):
                return
            e = i // 8
            cnt = P.run("gp", prepass_units[i], inc=f"pre{e}", n=16)
            prepass_idx[0] += 1
            if i % 8 == 7:
                prepass_done_at[e] = cnt

    def emit_phase_a(ii):
        ks_started = [False]
        kv_slot_ctr = [0]
        kv_slots = [(pb[4][:, :], "pb4"), (pkv6[:, :], "pkv6"), (pkv7[:, :], "pkv7")]
        for k in range(NTA):
            sl = k % 2
            P.wait("pe", *dx_xa[k])
            if k == 0:
                P.wait("pe", "dxc", ii["dx_rows"])
            epil = []
            for ts in range(2):
                for proj in ("k", "v"):
                    wsb = wk_bf if proj == "k" else wv_bf
                    brw = bkrow if proj == "k" else bvrow
                    dst = ktm if proj == "k" else vtm
                    for nh in range(2):
                        bi = (0 if proj == "k" else 2) + nh
                        key = f"pb{bi}"
                        mms = [lambda dc=dc, nh=nh, wsb=wsb, bi=bi, ts=ts, sl=sl:
                               t.matmul(pb[bi][:, :],
                                        xbA[sl][:, dc * TA + ts * 128:dc * TA + ts * 128 + 128],
                                        wsb[:, dc * D + nh * 512:dc * D + nh * 512 + 512],
                                        start=(dc == 0), stop=False)
                               for dc in range(DC)]
                        mms.append(lambda nh=nh, brw=brw, bi=bi:
                                   t.matmul(pb[bi][:, :], ones_row[:, 0:128],
                                            brw[:, nh * 512:nh * 512 + 512],
                                            start=False, stop=True))
                        pe_c = pe_group(key, mms)
                        if proj == "v":
                            P.wait("dve", "pe", pe_c)
                            dv = P.run("dve", lambda bi=bi, ts=ts, nh=nh, dst=dst:
                                       v.tensor_copy(dst[:, ts * D + nh * 512:ts * D + nh * 512 + 512],
                                                     pb[bi][:, :]), inc="dve")
                            release(key, "dve", dv)
                            epil.append(dv)
                        else:
                            P.wait("act", "pe", pe_c)
                            P.wait("act", "dve", elu_guard[0])
                            ac = P.run("act", lambda bi=bi: s.activation(
                                elu_m[:, :], pb[bi][:, :], AF.Exp), inc="act")
                            P.wait("dve", "act", ac)
                            dv = P.run("dve", lambda bi=bi: v.tensor_scalar(
                                elu_a[:, :], pb[bi][:, :], 0.0, None, ALU.max, ALU.bypass),
                                inc="dve")
                            release(key, "dve", dv)
                            dv = P.run("dve", lambda ts=ts, nh=nh, dst=dst:
                                       v.scalar_tensor_tensor(
                                           dst[:, ts * D + nh * 512:ts * D + nh * 512 + 512],
                                           elu_m[:, :], 1.0, elu_a[:, :], ALU.min, ALU.add),
                                       inc="dve")
                            elu_guard[0] = dv
                            epil.append(dv)
            for dvv in epil:
                P.wait("pe", "dve", dvv)
            for c in range(DC):
                for half in range(2):
                    si = kv_slot_ctr[0] % 3
                    kv_slot_ctr[0] += 1
                    bank_ap, key = kv_slots[si]
                    mms = [lambda c=c, half=half, ts=ts, bank_ap=bank_ap:
                           t.matmul(bank_ap,
                                    ktm[:, ts * D + c * 128:ts * D + c * 128 + 128],
                                    vtm[:, ts * D + half * 512:ts * D + half * 512 + 512],
                                    start=(ts == 0), stop=(ts == 1))
                           for ts in range(2)]
                    if half == 0:
                        for ts in range(2):
                            st = not ks_started[0]
                            ks_started[0] = True
                            last_ks = (k == NTA - 1 and c == DC - 1 and ts == 1)
                            mms.append(lambda c=c, ts=ts, st=st, last_ks=last_ks:
                                       t.matmul(pb[5][:, c:c + 1],
                                                ktm[:, ts * D + c * 128:ts * D + c * 128 + 128],
                                                ones_col[:, :], start=st, stop=last_ks))
                    pe_c = pe_group(key, mms)
                    P.wait("dve", "pe", pe_c)
                    dst = kv_acc[:, c * D + half * 512:c * D + half * 512 + 512]
                    if k == 0:
                        dv = P.run("dve", lambda dst=dst, bank_ap=bank_ap:
                                   v.tensor_copy(dst, bank_ap), inc="dve")
                    else:
                        dv = P.run("dve", lambda dst=dst, bank_ap=bank_ap:
                                   v.tensor_add(dst, dst, bank_ap), inc="dve")
                    release(key, "dve", dv)
            xa_slot_rel[sl] = P.cnt["pe"]
            if k + 2 < NTA:
                dx_xa.append(emit_load_xa(k + 2))
            emit_prepass(2)
        pe_c = P.cnt["pe"]
        P.wait("dve", "pe", pe_c)
        P.run("dve", lambda: v.tensor_scalar(ksf[:, :], pb[5][:, 0:8], 1e-6, None,
                                             ALU.add, ALU.bypass))
        dv = P.run("dve", lambda: v.tensor_copy(ksum_bf[:, :], ksf[:, :]), inc="dve")
        release("pb5", "dve", dv)
        for c in range(DC):
            dv = P.run("dve", lambda c=c: v.tensor_copy(
                kv_bf[:, c * D:(c + 1) * D], kv_acc[:, c * D:(c + 1) * D]), inc="dve")
        kv_ready = dv
        if debug:
            P.wait("gp", "dve", kv_ready)
            P.run("gp", lambda: g.dma_start(dbg["kv"], kv_acc[:, :]), inc="dbg", n=16)
            dbg_done = P.run("gp", lambda: g.dma_start(dbg["ksum"], ksf[:, :]), inc="dbg", n=16)
            dbg["a_done"] = dbg_done
        return kv_ready

    xb_rel = [0, 0]
    dx_xb = []

    def emit_load_xb(j):
        sl = j % 2
        P.wait("gp", "dve", xb_rel[sl])
        t0 = j * TB
        ch = f"dxb{sl}"
        return ch, P.run("gp", lambda t0=t0, sl=sl: g.dma_start(
            xbB[sl][:, :].rearrange("p (c n) -> p c n", c=DC),
            xTo[:, t0:t0 + TB].rearrange("(c p) n -> p c n", p=128)),
            inc=ch, n=16)

    def emit_phase_b(ii, kv_ready):
        P.wait("gp", "dve", kv_ready)
        dx_xb.append(emit_load_xb(0))
        dx_xb.append(emit_load_xb(1))
        P.wait("pe", "dve", kv_ready)
        P.wait("act", "dve", ii["bqn_done"])
        h_last_map = {}

        def emit_router(j, h_last):
                P.wait("pe", "dve", h_last)
                mms = []
                for tl in range(4):
                    tcol = j * TB + tl * 128
                    for c in range(DC):
                        mms.append(lambda c=c, tcol=tcol, tl=tl: t.matmul(
                            pb[5][:, tl * 8:tl * 8 + 8],
                            h_fm[:, c * TOWN + tcol:c * TOWN + tcol + 128],
                            wr_bf[:, c * 8:c * 8 + 8], start=(c == 0), stop=(c == DC - 1)))
                pe_c = pe_group("pb5", mms)
                P.wait("dve", "pe", pe_c)
                rel_list = []
                for tl in range(4):
                    dv = P.run("dve", lambda tl=tl: v.reduce_max(
                        rt_nm[:, tl:tl + 1], pb[5][:, tl * 8:tl * 8 + 8], axis=AX, negate=True),
                        inc="dve")
                    P.wait("act", "dve", dv)
                    ac = P.run("act", lambda tl=tl: s.activation(
                        rt_ex[:, tl * 8:tl * 8 + 8], pb[5][:, tl * 8:tl * 8 + 8], AF.Exp,
                        bias=rt_nm[:, tl:tl + 1]), inc="act")
                    P.wait("dve", "act", ac)
                    dv = P.run("dve", lambda tl=tl: v.reduce_sum(
                        rt_nm[:, 4 + tl:5 + tl], rt_ex[:, tl * 8:tl * 8 + 8], axis=AX), inc="dve")
                    dv = P.run("dve", lambda tl=tl: v.reciprocal(
                        rt_nm[:, 4 + tl:5 + tl], rt_nm[:, 4 + tl:5 + tl]), inc="dve")
                    dvq = P.run("dve", lambda tl=tl: v.tensor_scalar(
                        rt_g1[:, tl * 8:tl * 8 + 8], pb[5][:, tl * 8:tl * 8 + 8],
                        rt_nm[:, tl:tl + 1], 0.0, ALU.add, ALU.is_equal), inc="dve")
                    gcol0 = (j * 4 + tl) * 8
                    dv = P.run("dve", lambda gcol0=gcol0, tl=tl: v.tensor_scalar(
                        gate_tm[:, gcol0:gcol0 + 8], rt_g1[:, tl * 8:tl * 8 + 8],
                        rt_nm[:, 4 + tl:5 + tl], None, ALU.mult, ALU.bypass), inc="dve")
                    rel_list.append(("act", ac))
                    rel_list.append(("dve", dvq))
                bank_rel["pb5"] = rel_list[-2:] + [max(
                    (cv for cv in rel_list if cv[0] == "act"), key=lambda x: x[1]),
                    max((cv for cv in rel_list if cv[0] == "dve"), key=lambda x: x[1])]

        for j in range(NTB):
            sl = j % 2
            P.wait("pe", *dx_xb[j])
            q_last = 0
            for dq in range(DC):
                bi = dq % 2
                key = f"pb{bi}"
                mms = [lambda dc=dc, dq=dq, sl=sl, bi=bi:
                       t.matmul(pb[bi][:, :], wq_bf[:, dc * D + dq * 128:dc * D + dq * 128 + 128],
                                xbB[sl][:, dc * TB:dc * TB + 512],
                                start=(dc == 0), stop=(dc == DC - 1))
                       for dc in range(DC)]
                pe_c = pe_group(key, mms)
                P.wait("act", "pe", pe_c)
                P.wait("act", "dve", elu_guard[0])
                ac = P.run("act", lambda bi=bi, dq=dq: s.activation(
                    elu_m[:, :], pb[bi][:, :], AF.Exp, bias=bq_col[:, dq:dq + 1]), inc="act")
                P.wait("dve", "act", ac)
                dv = P.run("dve", lambda bi=bi, dq=dq: v.tensor_scalar(
                    elu_a[:, :], pb[bi][:, :], bq_col[:, dq:dq + 1], 0.0, ALU.add, ALU.max),
                    inc="dve")
                release(key, "dve", dv)
                dv = P.run("dve", lambda dq=dq: v.scalar_tensor_tensor(
                    qs_bf[:, dq * TB:dq * TB + 512], elu_m[:, :], 1.0, elu_a[:, :],
                    ALU.min, ALU.add), inc="dve")
                elu_guard[0] = dv
                q_last = dv
            P.wait("pe", "dve", q_last)
            pe_c = pe_group("pkv6", [
                lambda c=c: t.matmul(pkv6[0:1, :], ksum_bf[:, c:c + 1],
                                     qs_bf[:, c * TB:c * TB + 512],
                                     start=(c == 0), stop=(c == DC - 1)) for c in range(DC)])
            P.wait("dve", "pe", pe_c)
            dv = P.run("dve", lambda: v.reciprocal(r_t[:, :], pkv6[0:1, :]), inc="dve")
            release("pkv6", "dve", dv)
            zr_done = P.run("dve", lambda: v.tensor_copy(zrow_bf[:, :], r_t[:, :]), inc="dve")
            zb_done = None
            ctx_last = 0
            for dvc in range(DC):
                bi = 2 + dvc % 2
                key = f"pb{bi}"
                mms = [lambda c=c, dvc=dvc, bi=bi:
                       t.matmul(pb[bi][:, :], kv_bf[:, c * D + dvc * 128:c * D + dvc * 128 + 128],
                                qs_bf[:, c * TB:c * TB + 512],
                                start=(c == 0), stop=(c == DC - 1)) for c in range(DC)]
                pe_c = pe_group(key, mms)
                if dvc == 0:
                    P.wait("pe", "dve", zr_done)
                    pe_z = pe_group("pkv7", [lambda: t.matmul(
                        pkv7[:, :], ones_row[:, 0:128], zrow_bf[:, :],
                        start=True, stop=True)])
                    P.wait("dve", "pe", pe_z)
                    zb_done = P.run("dve", lambda: v.tensor_copy(zb_sb[:, :], pkv7[:, :]),
                                    inc="dve")
                    release("pkv7", "dve", zb_done)
                P.wait("dve", "pe", pe_c)
                dv = P.run("dve", lambda bi=bi, dvc=dvc: v.tensor_copy(
                    ctx_bf[:, dvc * TB:dvc * TB + 512], pb[bi][:, :]), inc="dve")
                release(key, "dve", dv)
                ctx_last = dv
            P.wait("pe", "dve", ctx_last)
            v_last = 0
            for do in range(DC):
                bi = 4 + do % 2
                key = f"pb{bi}"
                mms = [lambda c=c, do=do, bi=bi:
                       t.matmul(pb[bi][:, :], wo_bf[:, c * D + do * 128:c * D + do * 128 + 128],
                                ctx_bf[:, c * TB:c * TB + 512],
                                start=(c == 0), stop=(c == DC - 1)) for c in range(DC)]
                pe_c = pe_group(key, mms)
                P.wait("dve", "pe", pe_c)
                P.wait("dve", "dve", elu_guard[0])
                P.run("dve", lambda bi=bi: v.tensor_mul(elu_a[:, :], pb[bi][:, :], zb_sb[:, :]))
                dv = P.run("dve", lambda do=do, sl=sl: v.scalar_tensor_tensor(
                    v_bf[:, do * TB:do * TB + 512], elu_a[:, :], bo_col[:, do:do + 1],
                    xbB[sl][:, do * TB:do * TB + 512], ALU.add, ALU.add), inc="dve")
                release(key, "dve", dv)
                v_last = dv
            xb_rel[sl] = v_last
            if j + 2 < NTB:
                dx_xb.append(emit_load_xb(j + 2))
            P.wait("pe", "dve", v_last)
            pe_s = pe_group("pkv6", [
                lambda c=c: t.matmul(pkv6[0:1, :], ones_col[:, :],
                                     v_bf[:, c * TB:c * TB + 512],
                                     start=(c == 0), stop=(c == DC - 1)) for c in range(DC)])
            bank_wait("pkv7")
            v2_pe = [0]
            for w in range(2):
                P.wait("dve", "pe", v2_pe[0])
                for cc in range(4):
                    c = w * 4 + cc
                    dv = P.run("dve", lambda c=c, cc=cc: v.tensor_mul(
                        v2_bf[:, cc * 512:cc * 512 + 512],
                        v_bf[:, c * TB:c * TB + 512], v_bf[:, c * TB:c * TB + 512]), inc="dve")
                P.wait("pe", "dve", dv)
                for cc in range(4):
                    c = w * 4 + cc
                    pe_c = P.run("pe", lambda c=c, cc=cc: t.matmul(
                        pkv7[0:1, :], ones_col[:, :], v2_bf[:, cc * 512:cc * 512 + 512],
                        start=(c == 0), stop=(c == DC - 1)),
                        inc="pe" if cc == 3 else None)
                v2_pe[0] = pe_c
            pe_s2 = P.cnt["pe"]
            P.wait("dve", "pe", pe_s2)
            P.run("dve", lambda: v.tensor_scalar(r_m[:, :], pkv6[0:1, :], 1.0 / D, None,
                                                 ALU.mult, ALU.bypass))
            dv = P.run("dve", lambda: v.tensor_scalar(r_v[:, :], pkv7[0:1, :], 1.0 / D, None,
                                                      ALU.mult, ALU.bypass), inc="dve")
            release("pkv6", "dve", dv)
            release("pkv7", "dve", dv)
            P.wait("dve", "dve", dv)
            P.run("dve", lambda: v.tensor_mul(r_t[:, :], r_m[:, :], r_m[:, :]))
            dv = P.run("dve", lambda: v.tensor_sub(r_v[:, :], r_v[:, :], r_t[:, :]), inc="dve")
            P.wait("act", "dve", dv)
            P.run("act", lambda: s.activation(r_t[:, :], r_v[:, :], AF.Ln, bias=eps_col[0:1, :]))
            ac = P.run("act", lambda: s.activation(r_t[:, :], r_t[:, :], AF.Exp, scale=-0.5),
                       inc="act")
            P.wait("dve", "act", ac)
            dv = P.run("dve", lambda: v.tensor_copy(arow_bf[:, :], r_t[:, :]), inc="dve")
            P.wait("dve", "dve", dv)
            P.run("dve", lambda: v.tensor_mul(r_t[:, :], r_t[:, :], r_m[:, :]))
            dv2 = P.run("dve", lambda: v.tensor_scalar(brow_bf[:, :], r_t[:, :], -1.0, None,
                                                       ALU.mult, ALU.bypass), inc="dve")
            P.wait("pe", "dve", dv2)
            pe_group("pb2", [lambda: t.matmul(pb[2][:, :], ones_row[:, 0:128], arow_bf[:, :],
                                              start=True, stop=True)])
            pe_bb = pe_group("pb3", [lambda: t.matmul(pb[3][:, :], ones_row[:, 0:128],
                                                      brow_bf[:, :], start=True, stop=True)])
            P.wait("dve", "pe", pe_bb)
            P.wait("dve", "dve", elu_guard[0])
            h_last = 0
            for c in range(DC):
                P.run("dve", lambda c=c: v.tensor_mul(elu_a[:, :],
                                                      v_bf[:, c * TB:c * TB + 512], pb[2][:, :]))
                dv = P.run("dve", lambda c=c, j=j: v.tensor_add(
                    h_fm[:, c * TOWN + j * TB:c * TOWN + j * TB + 512], elu_a[:, :],
                    pb[3][:, :]), inc="dve")
                h_last = dv
            elu_guard[0] = h_last
            release("pb2", "dve", h_last)
            release("pb3", "dve", h_last)
            h_last_map[j] = h_last
            if j > 0:
                emit_router(j - 1, h_last_map[j - 1])
        emit_router(NTB - 1, h_last_map[NTB - 1])
        b_done = P.cnt["dve"]
        if debug:
            P.wait("gp", "dve", b_done)
            P.run("gp", lambda: g.dma_start(dbg["h"], h_fm[:, :]), inc="dbg", n=16)
            P.run("gp", lambda: g.dma_start(dbg["gate"], gate_tm[:, :]), inc="dbg", n=16)
        return b_done

    def emit_phase_c(b_done):
        pe_end_b = P.cnt["pe"]
        w1q = [(tt, e, fb) for tt in range(NTT) for e in range(E) for fb in range(4)]
        w2q = [(tt, e, dh, q) for tt in range(NTT) for e in range(E)
               for dh in range(2) for q in range(4)]
        w1_rel = [pe_end_b, pe_end_b]
        w2_rel = [pe_end_b] * 4
        w1_at = {}
        w2_at = {}
        w1_next = [0]
        w2_next = [0]

        def issue_w1():
            i = w1_next[0]
            if i >= len(w1q):
                return
            w1_next[0] += 1
            tt, e, fb = w1q[i]
            sl = i % 2
            ch = "dw1a" if sl == 0 else "dw1b"
            P.wait("act", "pe", w1_rel[sl])
            P.wait("act", f"pre{e}", prepass_done_at[e])
            c = P.run("act", lambda e=e, fb=fb, sl=sl: s.dma_start(
                w1sl[sl][:, :].rearrange("p (c f) -> p c f", c=DC),
                w1s[e, :, fb * 1024:(fb + 1) * 1024].rearrange("(c p) f -> p c f", p=128)),
                inc=ch, n=16, dma=True)
            w1_at[(tt, e, fb)] = (ch, c, sl)

        def issue_w2():
            i = w2_next[0]
            if i >= len(w2q):
                return
            w2_next[0] += 1
            tt, e, dh, q = w2q[i]
            sl = i % 4
            ch = "dw2" + "abcd"[sl]
            P.wait("sp", "pe", w2_rel[sl])
            P.wait("sp", f"pre{e}", prepass_done_at[e])
            c = P.run("sp", lambda e=e, dh=dh, q=q, sl=sl: sy.dma_start(
                w2sl[sl][:, :].rearrange("p (c d) -> p c d", c=8),
                w2s[e, q * 1024:(q + 1) * 1024, dh * 512:(dh + 1) * 512]
                .rearrange("(c p) d -> p c d", p=128)), inc=ch, n=16)
            w2_at[(tt, e, dh, q)] = (ch, c, sl)

        issue_w1()
        issue_w1()
        for _ in range(4):
            issue_w2()
        oslot_rel = [0, 0]

        def h1_phase(tt, e, inject=None):
            gelu_last = 0
            for fc in range(FCN):
                if inject and fc in (4, 12, 20, 28):
                    inject[(fc - 4) // 8]()
                fb = fc // 8
                fcl = fc % 8
                bi = fc % 2
                key = f"pb{bi}"
                wch, dwc, sl = w1_at[(tt, e, fb)]
                P.wait("pe", wch, dwc)
                mms = [lambda c=c, fcl=fcl, sl=sl, bi=bi, tt=tt: t.matmul(
                    pb[bi][:, :], w1sl[sl][:, c * 1024 + fcl * 128:c * 1024 + fcl * 128 + 128],
                    h_fm[:, c * TOWN + tt * TT:c * TOWN + tt * TT + 512],
                    start=(c == 0), stop=(c == DC - 1)) for c in range(DC)]
                pe_c = pe_group(key, mms)
                P.wait("act", "pe", pe_c)
                bcol = fc * 8 + e
                ac = P.run("act", lambda bi=bi, fc=fc, bcol=bcol: s.activation(
                    h1g[:, fc * 512:fc * 512 + 512], pb[bi][:, :], AF.Gelu,
                    bias=b1c[:, bcol:bcol + 1]), inc="act")
                release(key, "act", ac)
                gelu_last = ac
                if fcl == 7:
                    w1_rel[fb % 2] = P.cnt["pe"]
                    issue_w1()
            return gelu_last

        def out_phase(tt, e, out_rel, gelu_last):
            P.wait("pe", "act", gelu_last)
            for dh in range(2):
                for tsq in range(4):
                    ch, val = out_rel[tsq]
                    P.wait("pe", ch, val)
                for fk in range(FCN):
                    q = fk // 8
                    wch, dwc, sl = w2_at[(tt, e, dh, q)]
                    P.wait("pe", wch, dwc)
                    for tsq in range(4):
                        P.run("pe", lambda fk=fk, tsq=tsq, sl=sl: t.matmul(
                            pb[2 + tsq][:, :],
                            h1g[:, fk * 512 + tsq * 128:fk * 512 + tsq * 128 + 128],
                            w2sl[sl][:, (fk % 8) * 512:(fk % 8) * 512 + 512],
                            start=(fk == 0), stop=(fk == FCN - 1)),
                            inc="pe" if (fk % 8 == 7 and tsq == 3) else None)
                    if fk % 8 == 7:
                        _, _, sl_done = w2_at[(tt, e, dh, q)]
                        w2_rel[sl_done] = P.cnt["pe"]
                        issue_w2()
                pe_c = P.cnt["pe"]
                P.wait("dve", "pe", pe_c)
                for tsq in range(4):
                    gc = (tt * 4 + tsq) * 8 + e
                    dv = P.run("dve", lambda tsq=tsq, dh=dh, gc=gc: v.scalar_tensor_tensor(
                        acc[:, tsq * D + dh * 512:tsq * D + dh * 512 + 512],
                        pb[2 + tsq][:, :], gate_tm[:, gc:gc + 1],
                        acc[:, tsq * D + dh * 512:tsq * D + dh * 512 + 512],
                        ALU.mult, ALU.add), inc="dve")
                    out_rel[tsq] = ("dve", dv)
                    release(f"pb{2 + tsq}", "dve", dv)

        pending_ln2 = [None]
        for tt in range(NTT):
            if tt == 0:
                P.wait("pe", "dve", b_done)
            gelu_e0 = h1_phase(tt, 0, inject=pending_ln2[0])
            pending_ln2[0] = None
            if debug and tt == 0:
                P.wait("gp", "act", gelu_e0)
                _wch0, dwc0, sl0 = dw1_at[(0, 0)]
                P.run("gp", lambda sl0=sl0: g.dma_start(dbg["w1sl"], w1sl[sl0][:, :]),
                      inc="dbg", n=16)
                dxc = P.run("gp", lambda: g.dma_start(dbg["h1g"], h1g[:, :]), inc="dbg", n=16)
                P.wait("pe", "dbg", dxc)
            init_last = 0
            for ts in range(4):
                hsl = ts % 2
                gcol0 = (tt * 4 + ts) * 8
                dvg = P.run("dve", lambda gcol0=gcol0, ts=ts: v.tensor_copy(
                    rt_gb[:, ts * 8:ts * 8 + 8], gate_tm[:, gcol0:gcol0 + 8]), inc="dve")
                P.wait("pe", "dve", dvg)
                pe_c = pe_group("pb4", [lambda ts=ts: t.transpose(
                    pb[4][:, :].bitcast(BF16)[0:8, 0:128], rt_gb[:, ts * 8:ts * 8 + 8],
                    ident_bf[:, :])])
                P.wait("dve", "pe", pe_c)
                dvg = P.run("dve", lambda tt=tt, ts=ts: v.tensor_copy(
                    gate_fm[0:8, tt * TT + ts * 128:tt * TT + ts * 128 + 128],
                    pb[4][:, :].bitcast(BF16)[0:8, 0:128]), inc="dve")
                release("pb4", "dve", dvg)
                mms = [lambda c=c, tt=tt, ts=ts: t.transpose(
                    pkv6[:, :].bitcast(BF16)[:, c * 128:(c + 1) * 128],
                    h_fm[:, c * TOWN + tt * TT + ts * 128:c * TOWN + tt * TT + ts * 128 + 128],
                    ident_bf[:, :]) for c in range(DC)]
                pe_c = pe_group("pkv6", mms)
                P.wait("dve", "pe", pe_c)
                dvh = P.run("dve", lambda hsl=hsl: v.tensor_copy(
                    h_tm[hsl][:, :], pkv6[:, :].bitcast(BF16)[:, 0:1024]), inc="dve")
                release("pkv6", "dve", dvh)
                if debug and tt == 0 and ts == 0:
                    P.wait("gp", "dve", dvh)
                    P.run("gp", lambda: g.dma_start(dbg["hfm"], h_fm[:, :]), inc="dbg", n=16)
                    dxc = P.run("gp", lambda hsl=hsl: g.dma_start(dbg["htm"], h_tm[hsl][:, :]),
                                inc="dbg", n=16)
                    P.wait("dve", "dbg", dxc)
                P.wait("pe", "dve", dvg)
                for dh in range(2):
                    pe_c = pe_group("pkv7", [lambda tt=tt, ts=ts, dh=dh: t.matmul(
                        pkv7[:, :], gate_fm[0:8, tt * TT + ts * 128:tt * TT + ts * 128 + 128],
                        b2_sb[0:8, dh * 512:dh * 512 + 512], start=True, stop=True)])
                    P.wait("dve", "pe", pe_c)
                    dv2 = P.run("dve", lambda ts=ts, dh=dh, hsl=hsl: v.tensor_add(
                        acc[:, ts * D + dh * 512:ts * D + dh * 512 + 512],
                        h_tm[hsl][:, dh * 512:dh * 512 + 512], pkv7[:, :]), inc="dve")
                    release("pkv7", "dve", dv2)
                    init_last = dv2
            if debug and tt == 0:
                P.wait("gp", "dve", init_last)
                dxc = P.run("gp", lambda: g.dma_start(dbg["hin"], acc[:, :]), inc="dbg", n=16)
                P.wait("dve", "dbg", dxc)
            out_rel = {0: ("dve", init_last), 1: ("dve", init_last),
                       2: ("dve", init_last), 3: ("dve", init_last)}
            if cstub == "loads":
                for e in range(E):
                    load_w1(e, 0); load_w1(e, 1); load_w1(e, 2); load_w1(e, 3)
                    load_w2(e, 0); load_w2(e, 1); load_w2(e, 2); load_w2(e, 3)
                    w1_rel[0] = w1_rel[1] = P.cnt["pe"]
                    w2_rel[0] = w2_rel[1] = P.cnt["pe"]
            if cstub not in ("init", "loads"):
                out_phase(tt, 0, out_rel, gelu_e0)
                for e in range(1, E):
                    gl = h1_phase(tt, e)
                    out_phase(tt, e, out_rel, gl)
            if debug and tt == 0:
                P.wait("gp", "dve", out_rel[3][1])
                dxc = P.run("gp", lambda: g.dma_start(dbg["acc"], acc[:, :]), inc="dbg", n=16)
                P.wait("dve", "dbg", dxc)
            part2 = []
            for tsq in range(4):
                def mk_ln2(tt=tt, tsq=tsq):
                    def emitfn():
                        sl = tsq % 2
                        ob = (obuf0 if sl == 0 else obuf1)[:, :]
                        a_sl = acc[:, tsq * D:(tsq + 1) * D]
                        P.wait("dve", f"do{sl}", oslot_rel[sl])
                        P.run("dve", lambda: v.tensor_copy(ob, a_sl))
                        P.run("dve", lambda: v.scalar_tensor_tensor(
                            a_sl, ob, 1.0, ob, ALU.bypass, ALU.mult,
                            accum_out=ln2r[:, tsq * 4 + 1:tsq * 4 + 2]))
                        P.run("dve", lambda: v.tensor_scalar(
                            a_sl, ob, 0.0, None, ALU.add, ALU.add,
                            accum_out=ln2r[:, tsq * 4:tsq * 4 + 1]))
                        P.run("dve", lambda: v.tensor_scalar(
                            ln2r[:, tsq * 4:tsq * 4 + 1], ln2r[:, tsq * 4:tsq * 4 + 1],
                            1.0 / D, None, ALU.mult, ALU.bypass))
                        P.run("dve", lambda: v.tensor_scalar(
                            ln2r[:, tsq * 4 + 1:tsq * 4 + 2],
                            ln2r[:, tsq * 4 + 1:tsq * 4 + 2],
                            1.0 / D, None, ALU.mult, ALU.bypass))
                        P.run("dve", lambda: v.tensor_mul(
                            ln2r[:, tsq * 4 + 2:tsq * 4 + 3], ln2r[:, tsq * 4:tsq * 4 + 1],
                            ln2r[:, tsq * 4:tsq * 4 + 1]))
                        dv = P.run("dve", lambda: v.tensor_sub(
                            ln2r[:, tsq * 4 + 1:tsq * 4 + 2],
                            ln2r[:, tsq * 4 + 1:tsq * 4 + 2],
                            ln2r[:, tsq * 4 + 2:tsq * 4 + 3]), inc="dve")
                        P.wait("act", "dve", dv)
                        ac = P.run("act", lambda: s.activation(
                            ln2r[:, tsq * 4 + 2:tsq * 4 + 3],
                            ln2r[:, tsq * 4 + 1:tsq * 4 + 2],
                            AF.Sqrt, bias=eps_col[:, :]), inc="act")
                        P.wait("dve", "act", ac)
                        P.run("dve", lambda: v.reciprocal(
                            ln2r[:, tsq * 4 + 3:tsq * 4 + 4],
                            ln2r[:, tsq * 4 + 2:tsq * 4 + 3]))
                        dva = P.run("dve", lambda: v.tensor_scalar(
                            ob, ob, ln2r[:, tsq * 4:tsq * 4 + 1],
                            ln2r[:, tsq * 4 + 3:tsq * 4 + 4], ALU.subtract, ALU.mult),
                            inc="dve")
                        P.wait("sp", "dve", dva)
                        r0 = tt * TT + tsq * 128
                        oslot_rel[sl] = P.run("sp", lambda: sy.dma_start(
                            y[r0:r0 + 128, :], ob), inc=f"do{sl}", n=16)
                    return emitfn
                part2.append(mk_ln2())
            pending_ln2[0] = part2

        if pending_ln2[0]:
            for emitfn in pending_ln2[0]:
                emitfn()
            pending_ln2[0] = None

    ii = emit_init()
    kv_ready = emit_phase_a(ii)
    b_done = None
    if last_phase in ("B", "C"):
        b_done = emit_phase_b(ii, kv_ready)
    if last_phase == "C":
        if cstub == "dummy":
            P.wait("pe", "dve", b_done)
            for i in range(30000):
                P.run("pe", lambda: t.matmul(pb[0][:, :], ident_bf[:, :],
                                             wq_bf[:, 0:512], start=True, stop=True),
                      inc="pe" if i % 100 == 99 else None)
        elif cstub == "dummy2":
            pass
        else:
            emit_phase_c(b_done)
        emit_prepass(len(prepass_units))
    for tsq in range(4):
        P.wait("sp", f"do{tsq}", P.cnt[f"do{tsq}"])
    for ch in ("dxc", "dxa0", "dxa1", "dxb0", "dxb1", "dw1a", "dw1b", "dw2a",
               "dw2b", "dbg"):
        P.wait("sp", ch, P.cnt[ch])
    for e_ in range(E):
        P.wait("sp", f"pre{e_}", P.cnt[f"pre{e_}"])
    for eng_ in P.ENGS:
        P.flushw(eng_)

    with nc.Block() as block:
        @block.tensor
        def _(_):
            for fn in P.streams["pe"]:
                fn()

        @block.vector
        def _(_):
            for fn in P.streams["dve"]:
                fn()

        @block.scalar
        def _(_):
            for fn in P.streams["act"]:
                fn()

        @block.gpsimd
        def _(_):
            for fn in P.streams["gp"]:
                fn()

        @block.sync
        def _(_):
            for fn in P.streams["sp"]:
                fn()

    if cstub == "dummy2":
        with nc.Block() as block2:
            @block2.tensor
            def _(_):
                for i in range(30000):
                    t.matmul(pb[0][:, :], ident_bf[:, :], wq_bf[:, 0:512],
                             start=True, stop=True)

    return nc


_NC_CACHE = {}


def _get_nc(debug=False, last_phase="C"):
    key = (debug, last_phase)
    if key not in _NC_CACHE:
        _NC_CACHE[key] = build(debug=debug, last_phase=last_phase)
    return _NC_CACHE[key]


def make_inmaps(L0_attn_x, L0_attn_Wq, L0_attn_bq, L0_attn_Wk, L0_attn_bk,
                L0_attn_Wv, L0_attn_bv, L0_attn_Wo, L0_attn_bo,
                L0_ffn_Wr, L0_ffn_W1, L0_ffn_b1, L0_ffn_W2, L0_ffn_b2):
    c = np.ascontiguousarray
    f = lambda a: np.asarray(a, dtype=np.float32)
    shared = {
        "wqT": c(f(L0_attn_Wq).T), "wkT": c(f(L0_attn_Wk).T),
        "wvT": c(f(L0_attn_Wv).T), "woT": c(f(L0_attn_Wo).T),
        "bq": f(L0_attn_bq).reshape(1, D), "bk": f(L0_attn_bk).reshape(1, D),
        "bv": f(L0_attn_bv).reshape(1, D), "bo": f(L0_attn_bo).reshape(1, D),
        "wrT": c(f(L0_ffn_Wr).T),
        "w1": c(f(L0_ffn_W1)), "b1": c(f(L0_ffn_b1)),
        "w2": c(f(L0_ffn_W2)), "b2": c(f(L0_ffn_b2)),
    }
    in_maps = []
    for core in range(NCORE):
        b = core // 2
        h = core % 2
        xt = c(f(L0_attn_x)[b].T)
        m = dict(shared)
        m["xT"] = xt
        m["xTo"] = c(xt[:, h * TOWN:(h + 1) * TOWN])
        in_maps.append(m)
    return in_maps


def kernel(L0_attn_x, L0_attn_Wq, L0_attn_bq, L0_attn_Wk, L0_attn_bk,
           L0_attn_Wv, L0_attn_bv, L0_attn_Wo, L0_attn_bo,
           L0_ffn_Wr, L0_ffn_W1, L0_ffn_b1, L0_ffn_W2, L0_ffn_b2):
    in_maps = make_inmaps(L0_attn_x, L0_attn_Wq, L0_attn_bq, L0_attn_Wk, L0_attn_bk,
                          L0_attn_Wv, L0_attn_bv, L0_attn_Wo, L0_attn_bo,
                          L0_ffn_Wr, L0_ffn_W1, L0_ffn_b1, L0_ffn_W2, L0_ffn_b2)
    nc = _get_nc()
    res = run_bass_kernel_spmd(nc, in_maps, core_ids=list(range(NCORE)))
    out = np.empty((B, S, D), dtype=np.float32)
    for core in range(NCORE):
        b = core // 2
        h = core % 2
        out[b, h * TOWN:(h + 1) * TOWN, :] = res.results[core]["y"]
    return out
